# version 01 (final)
# baseline (speedup 1.0000x reference)
import numpy as np
from contextlib import ExitStack

import concourse.bass as bass
import concourse.mybir as mybir
from concourse.bass_utils import run_bass_kernel_spmd

F32 = mybir.dt.float32
BF16 = mybir.dt.bfloat16
AF = mybir.ActivationFunctionType
ALU = mybir.AluOpType

NCORES = 8
T = 2048
TC = 512
NKT = 8
EPS_LN = 1e-5

ENGS = ("sync", "tensor", "vector", "scalar", "gpsimd")
L1_ORDER = ["q", "k", "v"]
L1_DEBUG = False
L3_DEBUG = False


class Prog:

    def __init__(self, name="prog"):
        self.nc = bass.Bass("TRN2", target_bir_lowering=False)
        self.name = name
        self.q = {e: [] for e in ENGS}
        self.cnt = {}
        self.sems = {}
        self.ctx = ExitStack()
        self._built = False
        self._cconst = None
        self._cconst_vals = {}
        self._prelude_dv = 0
        self._first_op = {e: True for e in ENGS}

    def sb(self, name, shape, dtype):
        return self.ctx.enter_context(self.nc.sbuf_tensor(name, shape, dtype))

    def ps(self, name, shape, dtype=F32):
        return self.ctx.enter_context(self.nc.psum_tensor(name, shape, dtype))

    def sem(self, name):
        h = self.ctx.enter_context(self.nc.semaphore(name))
        self.sems[name] = h
        self.cnt[name] = 0
        return h

    def init_consts(self, values=(EPS_LN, 1.0, -1.0, 0.5, -0.5, 1e-6, 1e-8,
                                  1e-12, 2.0, 0.0, -80.0, 1.0 / 1024.0)):
        vals = list(dict.fromkeys(values))
        self._cconst = self.sb("cconsts", [128, len(vals)], F32)
        for i, v in enumerate(vals):
            self._cconst_vals[v] = i
            col = self._cconst[:, i:i + 1]
            self.op("vector", lambda e, _c=col, _v=v: e.memset(_c, _v), inc="dv")
            self.nc.const_aps.register(F32, v, col)
        self._prelude_dv = self.val("dv")

    def ccol(self, v):
        return self._cconst[:, self._cconst_vals[v]:self._cconst_vals[v] + 1]

    def din(self, name, shape, dtype):
        return self.nc.dram_tensor(name, shape, dtype, kind="ExternalInput").ap()

    def dout(self, name, shape, dtype):
        return self.nc.dram_tensor(name, shape, dtype, kind="ExternalOutput").ap()

    def dtmp(self, name, shape, dtype):
        return self.nc.dram_tensor(name, shape, dtype).ap()

    def add(self, eng, fn):
        self.q[eng].append(fn)

    def bump(self, sem_name, n=1):
        self.cnt[sem_name] += n
        return self.cnt[sem_name]

    def val(self, sem_name):
        return self.cnt[sem_name]

    def op(self, eng, op_fn, inc=None, inc_n=1, wait=None):
        v = self.bump(inc, inc_n) if inc else None
        w = [x for x in (wait or []) if x is not None and x[1] > 0]

        if self._first_op.get(eng) and eng != "vector" and self._prelude_dv:
            self._first_op[eng] = False
            w = [("dv", self._prelude_dv)] + w

        def fn(e, _op=op_fn, _w=w, _inc=inc, _n=inc_n):
            for (s, val) in _w:
                e.wait_ge(self.sems[s], val)
            inst = _op(e)
            if _inc is not None:
                inst.then_inc(self.sems[_inc], _n)
        self.add(eng, fn)
        return v

    def vop(self, op_fn, wait=None, inc="dv"):
        return self.op("vector", op_fn, inc=inc,
                       wait=[("dv", self.val("dv"))] + list(wait or []))

    def wait(self, eng, sem_name, value):
        self.add(eng, lambda e, _s=sem_name, _v=value: e.wait_ge(self.sems[_s], _v))

    def build(self):
        assert not self._built
        self._built = True
        block = self.ctx.enter_context(self.nc.Block())
        for ename, deco in (("sync", block.sync), ("tensor", block.tensor),
                            ("vector", block.vector), ("scalar", block.scalar),
                            ("gpsimd", block.gpsimd)):
            fns = self.q[ename]
            if not fns:
                continue

            def body(eng, _fns=fns):
                for f in _fns:
                    f(eng)
            deco(body)
        self.ctx.close()
        return self.nc


def ceil_div(a, b):
    return (a + b - 1) // b


def tchunks(tok, tc=TC):
    out = []
    c0 = 0
    while c0 < tok:
        rem = tok - c0
        if rem <= tc:
            out.append((c0, tok))
            break
        if rem < tc + 128:
            h = rem // 2
            out.append((c0, c0 + h))
            out.append((c0 + h, tok))
            break
        out.append((c0, c0 + tc))
        c0 += tc
    return out



def emit_load_fm(p, dram, tiles, rows, cols, sem="li", eng="sync", wait=None,
                 col0=0):
    n = ceil_div(rows, 128)
    for k in range(n):
        r0, r1 = k * 128, min((k + 1) * 128, rows)
        p.op(eng, lambda e, _k=k, _r0=r0, _r1=r1: e.dma_start(
            out=tiles[_k][0:_r1 - _r0, col0:col0 + cols], in_=dram[_r0:_r1, 0:cols]),
            inc=sem, inc_n=16, wait=wait if k == 0 else None)


def emit_store_fm(p, dram, tiles, rows, cols, sem="so", eng="gpsimd", wait=None):
    n = ceil_div(rows, 128)
    for k in range(n):
        r0, r1 = k * 128, min((k + 1) * 128, rows)
        p.op(eng, lambda e, _k=k, _r0=r0, _r1=r1: e.dma_start(
            out=dram[_r0:_r1, 0:cols], in_=tiles[_k][0:_r1 - _r0, 0:cols]),
            inc=sem, inc_n=16, wait=wait)


class LNScratch:

    def __init__(self, p, tag, tokens=T, sq_bufs=None, rows_buf=None,
                 absb_buf=None):
        self.ones = p.sb(f"{tag}_ones", [128, 1], F32)
        self.onesr = p.sb(f"{tag}_onesr", [1, 1], F32)
        self.sq = sq_bufs or [p.sb(f"{tag}_sq{i}", [128, TC], F32) for i in range(2)]
        self.rows = rows_buf if rows_buf is not None else p.sb(f"{tag}_rows", [1, 4 * TC], F32)
        self.p_stat = p.ps(f"{tag}_pst", [128, TC], F32)
        self.p_bc = p.ps(f"{tag}_pbc", [128, TC], F32)
        self.absb = absb_buf if absb_buf is not None else p.sb(f"{tag}_absb", [128, TC], F32)
        self.tokens = tokens
        self.stat_free = 0
        self.bc_free = 0
        self.sq_free = [0, 0]
        p.op("vector", lambda e: e.memset(self.ones[:], 1.0), inc="dv")
        p.op("vector", lambda e: e.memset(self.onesr[:], 1.0), inc="dv")


def emit_ln(p, ln, x_tiles, out_tiles, tokens, x_ready, gain_col=None,
            bias_col=None, nk=NKT, nfeat=1024.0, residual=False):
    nt = ceil_div(tokens, TC)
    rows = ln.rows
    mean = lambda c0, c1: rows[0:1, 0:c1 - c0]
    msq = lambda c0, c1: rows[0:1, TC:TC + c1 - c0]
    var = lambda c0, c1: rows[0:1, 2 * TC:2 * TC + c1 - c0]
    rstd = lambda c0, c1: rows[0:1, TC:TC + c1 - c0]
    arow = lambda c0, c1: rows[0:1, 3 * TC:3 * TC + c1 - c0]
    brow = lambda c0, c1: rows[0:1, 2 * TC:2 * TC + c1 - c0]

    for tci, (c0, c1) in enumerate(tchunks(tokens)):
        w = c1 - c0
        for k in range(nk):
            sq = ln.sq[k % 2]
            p.op("scalar", lambda e, _k=k, _c0=c0, _c1=c1, _sq=sq: e.activation(
                out=_sq[:, 0:_c1 - _c0], in_=x_tiles[_k][:, _c0:_c1], func=AF.Square),
                inc="ac",
                wait=(x_ready if (tci == 0 and k == 0) else []) +
                     ([("pe", ln.sq_free[k % 2])] if ln.sq_free[k % 2] else []) +
                     ([("dv", ln.bc_free)] if (k < 2 and ln.bc_free) else []))
            ac_v = p.val("ac")
            p.op("tensor", lambda e, _k=k, _c0=c0, _c1=c1: e.matmul(
                out=ln.p_stat[0:1, 0:_c1 - _c0], lhsT=ln.ones[:],
                rhs=x_tiles[_k][:, _c0:_c1],
                start=(_k == 0), stop=(_k == nk - 1)),
                inc=None,
                wait=(x_ready if (tci == 0 and k == 0) else []) +
                     ([("dv", ln.stat_free)] if (k == 0 and ln.stat_free) else []))
            p.op("tensor", lambda e, _k=k, _w=w, _sq=sq: e.matmul(
                out=ln.p_stat[64:65, 0:_w], lhsT=ln.ones[:], rhs=_sq[:, 0:_w],
                start=(_k == 0), stop=(_k == nk - 1)),
                inc="pe", wait=[("ac", ac_v)])
            ln.sq_free[k % 2] = p.val("pe")
        pe_done = p.val("pe")
        p.op("vector", lambda e, _c0=c0, _c1=c1: e.tensor_scalar_mul(
            out=mean(_c0, _c1), in0=ln.p_stat[0:1, 0:_c1 - _c0],
            scalar1=1.0 / nfeat),
            inc="dv", wait=[("pe", pe_done)])
        dvv = p.val("dv")
        p.op("scalar", lambda e, _c0=c0, _c1=c1: e.activation(
            out=msq(_c0, _c1), in_=mean(_c0, _c1), func=AF.Square),
            inc="ac", wait=[("dv", dvv)])
        acv = p.val("ac")
        p.op("vector", lambda e, _c0=c0, _c1=c1: e.scalar_tensor_tensor(
            out=var(_c0, _c1), in0=ln.p_stat[64:65, 0:_c1 - _c0],
            scalar=1.0 / nfeat, in1=msq(_c0, _c1),
            op0=ALU.mult, op1=ALU.subtract),
            inc="dv", wait=[("ac", acv)])
        ln.stat_free = p.val("dv")
        dvv = p.val("dv")
        p.op("scalar", lambda e, _c0=c0, _c1=c1: e.activation(
            out=rstd(_c0, _c1), in_=var(_c0, _c1), func=AF.Ln, bias=EPS_LN),
            inc=None, wait=[("dv", dvv)])
        p.op("scalar", lambda e, _c0=c0, _c1=c1: e.activation(
            out=arow(_c0, _c1), in_=rstd(_c0, _c1), func=AF.Exp, scale=-0.5),
            inc="ac")
        acv = p.val("ac")
        if residual:
            pass
        p.op("vector", lambda e, _c0=c0, _c1=c1: e.scalar_tensor_tensor(
            out=brow(_c0, _c1), in0=mean(_c0, _c1), scalar=-1.0,
            in1=arow(_c0, _c1), op0=ALU.mult, op1=ALU.mult),
            inc="dv", wait=[("ac", acv)])
        if residual:
            p.op("vector", lambda e, _c0=c0, _c1=c1: e.tensor_scalar_add(
                out=arow(_c0, _c1), in0=arow(_c0, _c1), scalar1=1.0), inc="dv")
        dvv = p.val("dv")
        p.op("tensor", lambda e, _c0=c0, _c1=c1: e.matmul(
            out=ln.p_bc[:, 0:_c1 - _c0],
            lhsT=ln.onesr[0:1, 0:1].broadcast_to((1, 128)),
            rhs=arow(_c0, _c1), start=True, stop=True),
            inc="pe",
            wait=[("dv", dvv)] + ([("dv", ln.bc_free)] if ln.bc_free else []))
        pev0 = p.val("pe")
        p.op("vector", lambda e, _c0=c0, _c1=c1: e.tensor_copy(
            out=ln.absb[:, 0:_c1 - _c0], in_=ln.p_bc[:, 0:_c1 - _c0]),
            inc="dv", wait=[("pe", pev0)])
        a_dv = p.val("dv")
        p.op("tensor", lambda e, _c0=c0, _c1=c1, _w=w: e.matmul(
            out=ln.p_bc[:, 0:_w],
            lhsT=ln.onesr[0:1, 0:1].broadcast_to((1, 128)),
            rhs=brow(_c0, _c1), start=True, stop=True),
            inc="pe", wait=[("dv", a_dv)])
        pev = p.val("pe")
        for k in range(nk):
            sq = ln.sq[k % 2]
            p.op("vector", lambda e, _k=k, _c0=c0, _c1=c1, _sq=sq: e.tensor_tensor(
                out=_sq[:, 0:_c1 - _c0], in0=x_tiles[_k][:, _c0:_c1],
                in1=ln.absb[:, 0:_c1 - _c0], op=ALU.mult),
                inc=None, wait=[("pe", pev)] if k == 0 else [])
            if gain_col is None:
                p.op("vector", lambda e, _k=k, _c0=c0, _c1=c1, _w=w, _sq=sq: e.tensor_tensor(
                    out=out_tiles[_k][:, _c0:_c1], in0=_sq[:, 0:_w],
                    in1=ln.p_bc[:, 0:_w], op=ALU.add),
                    inc="dv")
            else:
                p.op("vector", lambda e, _k=k, _c0=c0, _c1=c1, _w=w, _sq=sq: e.tensor_tensor(
                    out=_sq[:, 0:_w], in0=_sq[:, 0:_w],
                    in1=ln.p_bc[:, 0:_w], op=ALU.add),
                    inc=None)
                p.op("vector", lambda e, _k=k, _c0=c0, _c1=c1, _w=w, _sq=sq: e.tensor_scalar(
                    out=out_tiles[_k][:, _c0:_c1], in0=_sq[:, 0:_w],
                    scalar1=gain_col[_k], scalar2=bias_col[_k],
                    op0=ALU.mult, op1=ALU.add),
                    inc="dv")
        ln.bc_free = p.val("dv")
    return p.val("dv")


def emit_proj(p, tag, lhs_tiles_fn, rhs_tiles, nk, nof, tokens, epilogue,
              psum_pool, rhs_ready=None, m_width=128, extra_acc=None,
              tc=TC):
    rhs_fn = rhs_tiles if callable(rhs_tiles) else (
        lambda k, c0, c1: rhs_tiles[k][:, c0:c1])
    npool = len(psum_pool)
    idx = 0
    for of in range(nof):
        for tci, (c0, c1) in enumerate(tchunks(tokens, tc)):
            slot = idx % npool
            idx += 1
            ps_t, free_waits = psum_pool[slot]
            first_wait = list(free_waits) + list(rhs_ready or [])
            has_extra = extra_acc is not None
            for k in range(nk):
                p.op("tensor", lambda e, _k=k, _of=of, _c0=c0, _c1=c1, _ps=ps_t,
                     _st=(k == 0), _sp=(k == nk - 1 and not has_extra): e.matmul(
                    out=_ps[0:m_width, 0:_c1 - _c0],
                    lhsT=lhs_tiles_fn(_k, _of),
                    rhs=rhs_fn(_k, _c0, _c1),
                    start=_st, stop=_sp),
                    inc="pe" if (k == nk - 1 and not has_extra) else None,
                    wait=first_wait if k == 0 else [])
            if has_extra:
                extra_acc(of, tci, c0, c1, ps_t)
            pev = p.val("pe")
            frees = epilogue(of, tci, c0, c1, ps_t[0:m_width, 0:c1 - c0], pev)
            psum_pool[slot] = [ps_t, frees]


def emit_stream_weights(p, dram_list, stage_bufs, slot_sems=("lw0", "lw1"),
                        eng="sync", start_wait=None):
    cast_done = {}
    for i, item in enumerate(dram_list):
        if len(item) == 5:
            dram, r0, nr, cols, dst = item
            cc0 = 0
        else:
            dram, r0, nr, cc0, cols, dst = item
        buf = stage_bufs[i % 2]
        sem = slot_sems[i % 2]
        prev = cast_done.get(i - 2)
        p.op(eng, lambda e, _d=dram, _r0=r0, _nr=nr, _cc0=cc0, _c=cols, _b=buf: e.dma_start(
            out=_b[0:_nr, 0:_c], in_=_d[_r0:_r0 + _nr, _cc0:_cc0 + _c]),
            inc=sem, inc_n=16,
            wait=([("dv", prev)] if prev else []) + ((start_wait or []) if i == 0 else []))
        liv = p.val(sem)
        p.op("vector", lambda e, _b=buf, _dst=dst, _nr=nr, _c=cols: e.tensor_copy(
            out=_dst[0:_nr, 0:_c], in_=_b[0:_nr, 0:_c]), inc="dv",
            wait=[(sem, liv)])
        cast_done[i] = p.val("dv")
    return p.val("dv")



def build_L1():
    p = Prog("L1")
    xT = p.din("xT", [1024, T], F32)
    wq = p.din("wqT", [1024, 1024], F32)
    wk = p.din("wkT", [1024, 1024], F32)
    wv = p.din("wvT", [1024, 1024], F32)
    bq = p.din("bq", [128, 8], F32)
    bk = p.din("bk", [128, 8], F32)
    bv = p.din("bv", [128, 8], F32)
    oq = p.dout("QfT", [1024, T], BF16)
    ok = p.dout("KfT", [1024, T], BF16)
    ov = p.dout("VT", [1024, T], BF16)

    for s in ("li", "lw0", "lw1", "lb", "pe", "dv", "ac", "so"):
        p.sem(s)
    p.init_consts()

    xt = [p.sb(f"xt{k}", [128, T], F32) for k in range(NKT)]
    xn = [p.sb(f"xn{k}", [128, T], BF16) for k in range(NKT)]
    stage = [p.sb(f"stg{i}", [128, 1024], F32) for i in range(2)]
    wb = {n: [p.sb(f"w{n}b{k}", [128, 1024], BF16) for k in range(NKT)]
          for n in ("q", "k", "v")}
    bcols = {n: p.sb(f"b{n}c", [128, 8], F32) for n in ("q", "k", "v")}
    obuf = [p.sb(f"ob{k}", [128, T], BF16) for k in range(NKT)]
    scr = [p.sb(f"scr{i}", [128, TC], F32) for i in range(2)]
    ln = LNScratch(p, "ln")

    emit_load_fm(p, xT, xt, 1024, T)
    x_ready = [("li", p.val("li"))]

    wlist = []
    for wd, name in ((wq, "q"), (wk, "k"), (wv, "v")):
        for k in range(NKT):
            wlist.append((wd, k * 128, 128, 1024, wb[name][k]))
    emit_stream_weights(p, wlist, stage)
    for wd, name in ((bq, "q"), (bk, "k"), (bv, "v")):
        p.op("sync", lambda e, _wd=wd, _n=name: e.dma_start(
            out=bcols[_n][:], in_=_wd[:]), inc="lb", inc_n=16)
    w_ready = [("dv", p.val("dv")), ("lb", p.val("lb"))]

    emit_ln(p, ln, xt, xn, T, x_ready)
    xn_ready = [("dv", p.val("dv"))]

    pool = [[p.ps(f"pp{i}", [128, TC], F32), []] for i in range(4)]

    def mk_epi(kind, bcol, out_tiles, obuf_free):
        def epi(of, tci, c0, c1, ps_ap, pev):
            w = c1 - c0
            if kind == "v":
                s0, s1 = scr
                p.op("vector", lambda e: e.tensor_scalar_add(
                    out=s0[:, 0:w], in0=ps_ap,
                    scalar1=bcol[:, of:of + 1]), inc="dv",
                    wait=[("pe", pev), ("dv", p.val("dv"))] + obuf_free)
                p.op("vector", lambda e: e.tensor_copy(
                    out=out_tiles[of][:, c0:c1], in_=s0[:, 0:w]), inc="dv")
                return [("dv", p.val("dv"))]
            s0, s1 = scr
            p.op("vector", lambda e: e.tensor_scalar_add(
                out=s0[:, 0:w], in0=ps_ap, scalar1=bcol[:, of:of + 1]),
                inc="dv", wait=[("pe", pev), ("dv", p.val("dv"))])
            p.op("vector", lambda e: e.tensor_scalar_min(
                out=s1[:, 0:w], in0=s0[:, 0:w], scalar1=0.0),
                inc="dv")
            dvv = p.val("dv")
            p.op("scalar", lambda e: e.activation(
                out=s1[:, 0:w], in_=s1[:, 0:w], func=AF.Exp),
                inc="ac", wait=[("dv", dvv)])
            acv = p.val("ac")
            p.op("vector", lambda e: e.scalar_tensor_tensor(
                out=out_tiles[of][:, c0:c1], in0=s0[:, 0:w], scalar=0.0,
                in1=s1[:, 0:w], op0=ALU.max, op1=ALU.add),
                inc="dv", wait=[("ac", acv)] + obuf_free)
            return [("dv", p.val("dv"))]
        return epi

    plist = {"v": ("v", ov, "v"), "q": ("q", oq, "q"), "k": ("k", ok, "k")}
    for name, dram, kind in [plist[n] for n in L1_ORDER]:
        stores_done = [("so", p.val("so"))]
        emit_proj(p, f"pj{name}",
                  lambda k, of, _n=name: wb[_n][k][:, of * 128:(of + 1) * 128],
                  xn, NKT, NKT, T, mk_epi(kind, bcols[name], obuf, stores_done),
                  pool, rhs_ready=xn_ready + w_ready)
        done = [("dv", p.val("dv")), ("ac", p.val("ac"))]
        emit_store_fm(p, dram, obuf, 1024, T, wait=done)
    if L1_DEBUG:
        dxn = p.dout("dbg_xn", [1024, T], BF16)
        dwv = p.dout("dbg_wv", [1024, 1024], BF16)
        dall = [("dv", p.val("dv")), ("ac", p.val("ac")), ("pe", p.val("pe"))]
        emit_store_fm(p, dxn, xn, 1024, T, wait=dall)
        for k in range(NKT):
            p.op("gpsimd", lambda e, _k=k: e.dma_start(
                out=dwv[_k * 128:(_k + 1) * 128, :], in_=wb["v"][_k][:]),
                inc="so", inc_n=16, wait=dall)
    p.wait("gpsimd", "so", p.val("so"))
    return p.build()


class GateBufs:

    def __init__(self, p, gwb, lbcol, ey8, ones):
        self.gwb, self.lbcol, self.ey8, self.ones = gwb, lbcol, ey8, ones
        self.lg = p.sb("g_lg", [8, T], F32)
        self.sctm = p.sb("g_sctm", [128, 8], F32)
        self.mtm = p.sb("g_mtm", [128, 8], F32)
        self.wnm = p.sb("g_wnm", [128, 8], F32)
        self.t8 = p.sb("g_t8", [128, 8], F32)
        self.crow = p.sb("g_crow", [128, 4], F32)
        self.gp = p.sb("g_gp", [8, 2], F32)
        self.pg = p.ps("g_pg", [8, TC], F32)
        self.pst = p.ps("g_pst", [128, 8], F32)
        self.pcnt = p.ps("g_pcnt", [128, 1], F32)


def emit_gate(p, gb, xn_tiles, xn_ready, m_out, gp_out):
    lg, sctm, mtm, wnm, t8, crow = gb.lg, gb.sctm, gb.mtm, gb.wnm, gb.t8, gb.crow
    for tci in range(4):
        c0, c1 = tci * TC, (tci + 1) * TC
        for k in range(NKT):
            p.op("tensor", lambda e, _k=k, _c0=c0, _c1=c1: e.matmul(
                out=gb.pg[:, 0:_c1 - _c0], lhsT=gb.gwb[_k][:],
                rhs=xn_tiles[_k][:, _c0:_c1],
                start=(_k == 0), stop=(_k == NKT - 1)),
                inc="pe" if k == NKT - 1 else None,
                wait=(xn_ready + [("dv", p.val("dv"))]) if k == 0 else [])
        pev = p.val("pe")
        p.op("vector", lambda e, _c0=c0, _c1=c1: e.tensor_scalar_add(
            out=lg[:, _c0:_c1], in0=gb.pg[:, 0:_c1 - _c0], scalar1=gb.lbcol[:]),
            inc="dv", wait=[("pe", pev)])
    dvv = p.val("dv")
    p.op("scalar", lambda e: e.activation(out=lg[:], in_=lg[:], func=AF.Sigmoid),
         inc="ac", wait=[("dv", dvv)])
    sc_ready = p.val("ac")

    NTCH = T // 128
    for ci in range(NTCH):
        c0 = ci * 128
        p.op("tensor", lambda e, _c0=c0: e.transpose(
            out=gb.pst[:], in_=lg[0:8, _c0:_c0 + 128], identity=gb.ey8[:]),
            inc="pe", wait=[("ac", sc_ready)] + [("dv", p.val("dv"))])
        pev = p.val("pe")
        p.vop(lambda e: e.tensor_copy(out=sctm[:], in_=gb.pst[:]),
              wait=[("pe", pev)])
        for ee in range(8):
            p.vop(lambda e, _e=ee: e.tensor_single_scalar(
                out=t8[:], in_=sctm[:], scalar=sctm[:, _e:_e + 1], op=ALU.is_gt))
            p.vop(lambda e: e.reduce_sum(
                out=crow[:, 0:1], in_=t8[:], axis=mybir.AxisListType.X))
            if ee > 0:
                p.vop(lambda e, _e=ee: e.tensor_single_scalar(
                    out=t8[:, 0:_e], in_=sctm[:, 0:_e], scalar=sctm[:, _e:_e + 1],
                    op=ALU.is_equal))
                p.vop(lambda e, _e=ee: e.reduce_sum(
                    out=crow[:, 1:2], in_=t8[:, 0:_e], axis=mybir.AxisListType.X))
                p.vop(lambda e: e.tensor_tensor(
                    out=crow[:, 0:1], in0=crow[:, 0:1], in1=crow[:, 1:2],
                    op=ALU.add))
            p.vop(lambda e, _e=ee: e.tensor_single_scalar(
                out=mtm[:, _e:_e + 1], in_=crow[:, 0:1], scalar=2.0, op=ALU.is_lt))
        p.vop(lambda e: e.tensor_tensor(
            out=wnm[:], in0=mtm[:], in1=sctm[:], op=ALU.mult))
        p.vop(lambda e: e.reduce_sum(
            out=crow[:, 2:3], in_=wnm[:], axis=mybir.AxisListType.X))
        p.vop(lambda e: e.tensor_scalar_add(
            out=crow[:, 2:3], in0=crow[:, 2:3], scalar1=1e-8))
        p.vop(lambda e: e.reciprocal(
            out=crow[:, 3:4], in_=crow[:, 2:3]))
        p.vop(lambda e: e.tensor_single_scalar(
            out=wnm[:], in_=wnm[:], scalar=crow[:, 3:4], op=ALU.mult))
        mdv = p.val("dv")
        p.op("tensor", lambda e, _ci=ci: e.matmul(
            out=gb.pcnt[0:8, 0:1], lhsT=mtm[:], rhs=gb.ones[:],
            start=(_ci == 0), stop=(_ci == NTCH - 1)),
            inc=None, wait=[("dv", mdv)])
        p.op("tensor", lambda e, _ci=ci: e.matmul(
            out=gb.pcnt[64:72, 0:1], lhsT=wnm[:], rhs=gb.ones[:],
            start=(_ci == 0), stop=(_ci == NTCH - 1)),
            inc="pe")
        p.op("gpsimd", lambda e, _c0=c0: e.dma_start(
            out=m_out[_c0:_c0 + 128, :], in_=mtm[:]), inc="so", inc_n=16,
            wait=[("dv", mdv)])
        p.wait("vector", "so", p.val("so"))
    pev = p.val("pe")
    p.vop(lambda e: e.tensor_copy(out=gb.gp[:, 0:1], in_=gb.pcnt[0:8, 0:1]),
          wait=[("pe", pev)])
    p.vop(lambda e: e.tensor_copy(out=gb.gp[:, 1:2], in_=gb.pcnt[64:72, 0:1]))
    gp_dv = p.val("dv")
    p.op("gpsimd", lambda e: e.dma_start(out=gp_out[:, :], in_=gb.gp[:]),
         inc="so", inc_n=16, wait=[("dv", gp_dv)])


def build_L2():
    p = Prog("L2")
    S = 4096
    kf = p.din("Kf_tm", [S, 512], BF16)
    vv = p.din("V_tm", [S, 512], BF16)
    qf = p.din("Qf_fm", [512, S], BF16)
    eye8 = p.din("eye8", [8, 8], F32)
    oT = p.dout("oT", [512, S], BF16)
    for s in ("li", "pe", "dv", "so"):
        p.sem(s)
    p.init_consts()

    kft = [p.sb(f"kft{i}", [128, 512], BF16) for i in range(32)]
    vt = [p.sb(f"vt{i}", [128, 512], BF16) for i in range(32)]
    qft = [p.sb(f"qft{i}", [128, S], BF16) for i in range(4)]
    ot = [p.sb(f"ot{i}", [128, S], BF16) for i in range(4)]
    kvsb = [p.sb(f"kvsb{i}", [128, 64], BF16) for i in range(4)]
    ksr = p.sb("ksr", [1, 512], F32)
    ksc = p.sb("ksc", [128, 8], BF16)
    zrow = p.sb("zrow", [1, S], F32)
    zb = p.sb("zb", [64, S], F32)
    ones = p.sb("onesL2", [128, 1], F32)
    ones_bf = p.sb("onesL2b", [128, 1], BF16)
    ey8 = p.sb("ey8", [8, 8], F32)
    p.op("vector", lambda e: e.memset(ones[:], 1.0), inc="dv")
    p.op("vector", lambda e: e.memset(ones_bf[:], 1.0), inc="dv")

    pkv = p.ps("pkv", [128, 64], F32)
    pks = p.ps("pks", [1, 512], F32)
    ptr = p.ps("ptr", [64, 8], F32)
    pz = p.ps("pz", [1, 512], F32)
    pzb = p.ps("pzb", [64, 512], F32)
    po = p.ps("po", [64, 512], F32)

    for i in range(32):
        p.op("sync", lambda e, _i=i: e.dma_start(
            out=kft[_i][:], in_=kf[_i * 128:(_i + 1) * 128, :]), inc="li", inc_n=16)
        p.op("sync", lambda e, _i=i: e.dma_start(
            out=vt[_i][:], in_=vv[_i * 128:(_i + 1) * 128, :]), inc="li", inc_n=16)
    for i in range(4):
        p.op("sync", lambda e, _i=i: e.dma_start(
            out=qft[_i][:], in_=qf[_i * 128:(_i + 1) * 128, :]), inc="li", inc_n=16)
    p.op("sync", lambda e: e.dma_start(out=ey8[:], in_=eye8[:, :]), inc="li", inc_n=16)
    lall = [("li", p.val("li"))]

    for i in range(32):
        p.op("tensor", lambda e, _i=i: e.matmul(
            out=pks[:], lhsT=ones_bf[:], rhs=kft[_i][:],
            start=(_i == 0), stop=(_i == 31)),
            inc="pe" if i == 31 else None, wait=lall + [("dv", p.val("dv"))] if i == 0 else [])
    pev = p.val("pe")
    p.op("vector", lambda e: e.tensor_copy(out=ksr[:], in_=pks[:]), inc="dv",
         wait=[("pe", pev)])
    ks_dv = p.val("dv")
    for h in range(8):
        p.op("tensor", lambda e, _h=h: e.transpose(
            out=ptr[0:64, _h:_h + 1], in_=ksr[0:1, _h * 64:(_h + 1) * 64],
            identity=ey8[0:1, 0:1]),
            inc="pe" if h == 7 else None, wait=[("dv", ks_dv)] if h == 0 else [])
    pev = p.val("pe")
    for h in range(8):
        b0 = (h % 2) * 64
        p.op("vector", lambda e, _h=h, _b0=b0: e.tensor_copy(
            out=ksc[_b0:_b0 + 64, _h:_h + 1], in_=ptr[0:64, _h:_h + 1]),
            inc="dv", wait=[("pe", pev)] if h == 0 else [])
    ksc_dv = p.val("dv")

    for j in range(4):
        for hh in range(2):
            h = 2 * j + hh
            base = hh * 64
            for i in range(32):
                p.op("tensor", lambda e, _i=i, _h=h, _b=base: e.matmul(
                    out=pkv[_b:_b + 64, :],
                    lhsT=kft[_i][:, _h % 8 * 64:(_h % 8) * 64 + 64][:, 0:64] if False else kft[_i][:, (_h) * 64:(_h) * 64 + 64],
                    rhs=vt[_i][:, (_h) * 64:(_h) * 64 + 64],
                    start=(_i == 0), stop=(_i == 31)),
                    inc="pe" if i == 31 else None,
                    wait=(lall + [("dv", p.val("dv"))]) if i == 0 else [])
        pev = p.val("pe")
        p.op("vector", lambda e, _j=j: e.tensor_copy(out=kvsb[_j][:], in_=pkv[:]),
             inc="dv", wait=[("pe", pev)])
    kv_dv = p.val("dv")

    pz_free = 0
    pzb_free = 0
    po_free = 0
    for h in range(8):
        qrhs = lambda c0, c1, _h=h: qft[_h // 2][(_h % 2) * 64:(_h % 2) * 64 + 64, c0:c1]
        for ci in range(8):
            c0, c1 = ci * 512, (ci + 1) * 512
            p.op("tensor", lambda e, _c0=c0, _c1=c1, _h=h, _q=qrhs: e.matmul(
                out=pz[0:1, 0:_c1 - _c0],
                lhsT=ksc[(_h % 2) * 64:(_h % 2) * 64 + 64, _h:_h + 1],
                rhs=_q(_c0, _c1),
                start=True, stop=True),
                inc="pe", wait=[("dv", max(ksc_dv, pz_free))])
            pev = p.val("pe")
            p.op("vector", lambda e, _c0=c0, _c1=c1: e.tensor_scalar_add(
                out=zrow[0:1, _c0:_c1], in0=pz[0:1, 0:_c1 - _c0], scalar1=1e-6),
                inc="dv", wait=[("pe", pev)])
            pz_free = p.val("dv")
        p.op("vector", lambda e: e.reciprocal(out=zrow[:], in_=zrow[:]), inc="dv")
        zr_dv = p.val("dv")
        for ci in range(8):
            c0, c1 = ci * 512, (ci + 1) * 512
            p.op("tensor", lambda e, _c0=c0, _c1=c1: e.matmul(
                out=pzb[:, 0:_c1 - _c0],
                lhsT=ones[0:1, 0:1].broadcast_to((1, 64)),
                rhs=zrow[0:1, _c0:_c1], start=True, stop=True),
                inc="pe", wait=[("dv", max(zr_dv, pzb_free))])
            pev = p.val("pe")
            p.op("vector", lambda e, _c0=c0, _c1=c1: e.tensor_copy(
                out=zb[:, _c0:_c1], in_=pzb[:, 0:_c1 - _c0]), inc="dv",
                wait=[("pe", pev)])
            pzb_free = p.val("dv")
        for ci in range(8):
            c0, c1 = ci * 512, (ci + 1) * 512
            p.op("tensor", lambda e, _c0=c0, _c1=c1, _h=h, _q=qrhs: e.matmul(
                out=po[:, 0:_c1 - _c0],
                lhsT=kvsb[_h // 2][(_h % 2) * 64:(_h % 2) * 64 + 64, :],
                rhs=_q(_c0, _c1), start=True, stop=True),
                inc="pe", wait=[("dv", max(kv_dv, po_free))])
            pev = p.val("pe")
            p.op("vector", lambda e, _c0=c0, _c1=c1, _h=h: e.tensor_tensor(
                out=ot[_h // 2][(_h % 2) * 64:(_h % 2) * 64 + 64, _c0:_c1],
                in0=po[:, 0:_c1 - _c0], in1=zb[:, _c0:_c1], op=ALU.mult),
                inc="dv", wait=[("pe", pev)])
            po_free = p.val("dv")

    done = [("dv", p.val("dv"))]
    for i in range(4):
        p.op("gpsimd", lambda e, _i=i: e.dma_start(
            out=oT[_i * 128:(_i + 1) * 128, :], in_=ot[_i][:]),
            inc="so", inc_n=16, wait=done)
    p.wait("gpsimd", "so", p.val("so"))
    return p.build()


def build_L3():
    p = Prog("L3")
    xT = p.din("xT", [1024, T], F32)
    oT = p.din("oT", [1024, T], BF16)
    wo = p.din("woT", [1024, 1024], F32)
    bo = p.din("bo", [128, 8], F32)
    gw = p.din("gwT", [1024, 8], F32)
    lb = p.din("lb", [8, 1], F32)
    eye8 = p.din("eye8", [8, 8], F32)
    x1o = p.dout("x1T", [1024, T], F32)
    xn2o = p.dout("xn2T", [1024, T], BF16)
    mo = p.dout("m_tm", [T, 8], F32)
    gpo = p.dout("gpart", [8, 2], F32)
    for s in ("li", "lw0", "lw1", "lb2", "pe", "dv", "ac", "so"):
        p.sem(s)
    p.init_consts()

    xt = [p.sb(f"xt{k}", [128, T], F32) for k in range(NKT)]
    ott = [p.sb(f"ot{k}", [128, T], BF16) for k in range(NKT)]
    x1 = xt
    xn2 = [p.sb(f"xn2_{k}", [128, T], BF16) for k in range(NKT)]
    stage = [p.sb(f"stg{i}", [128, 1024], F32) for i in range(2)]
    wob = [p.sb(f"wob{k}", [128, 1024], BF16) for k in range(NKT)]
    gwb = [p.sb(f"gwb{k}", [128, 8], BF16) for k in range(NKT)]
    bocol = p.sb("bocol", [128, 8], F32)
    lbcol = p.sb("lbcol", [8, 1], F32)
    ey8 = p.sb("ey8", [8, 8], F32)
    scr = [p.sb(f"scr{i}", [128, TC], F32) for i in range(2)]
    ones = p.sb("onesL3", [128, 1], F32)
    p.op("vector", lambda e: e.memset(ones[:], 1.0), inc="dv")
    gb = GateBufs(p, gwb, lbcol, ey8, ones)
    ln = LNScratch(p, "ln")

    emit_load_fm(p, xT, xt, 1024, T, sem="li")
    emit_load_fm(p, oT, ott, 1024, T, sem="li")
    wlist = [(wo, k * 128, 128, 1024, wob[k]) for k in range(NKT)]
    wlist += [(gw, k * 128, 128, 8, gwb[k]) for k in range(NKT)]
    emit_stream_weights(p, wlist, stage)
    p.op("sync", lambda e: e.dma_start(out=bocol[:], in_=bo[:, :]), inc="lb2", inc_n=16)
    p.op("sync", lambda e: e.dma_start(out=lbcol[:], in_=lb[:, :]), inc="lb2", inc_n=16)
    p.op("sync", lambda e: e.dma_start(out=ey8[:], in_=eye8[:, :]), inc="lb2", inc_n=16)
    lall = [("li", p.val("li")), ("lb2", p.val("lb2")), ("dv", p.val("dv"))]

    pool = [[p.ps(f"pp{i}", [128, TC], F32), []] for i in range(2)]

    def epi_wo(of, tci, c0, c1, ps_ap, pev):
        w = c1 - c0
        s0 = scr[tci % 2]
        p.op("vector", lambda e: e.tensor_scalar_add(
            out=s0[:, 0:w], in0=ps_ap, scalar1=bocol[:, of:of + 1]),
            inc="dv", wait=[("pe", pev), ("dv", p.val("dv"))])
        p.op("vector", lambda e: e.tensor_tensor(
            out=x1[of][:, c0:c1], in0=s0[:, 0:w], in1=xt[of][:, c0:c1],
            op=ALU.add), inc="dv")
        return [("dv", p.val("dv"))]

    emit_proj(p, "wo", lambda k, of: wob[k][:, of * 128:(of + 1) * 128],
              ott, NKT, NKT, T, epi_wo, pool, rhs_ready=lall)

    emit_ln(p, ln, x1, xn2, T, [("dv", p.val("dv"))])
    xn_ready = [("dv", p.val("dv"))]

    emit_gate(p, gb, xn2, xn_ready, mo, gpo)

    done = [("dv", p.val("dv")), ("ac", p.val("ac"))]
    emit_store_fm(p, x1o, x1, 1024, T, wait=done)
    emit_store_fm(p, xn2o, xn2, 1024, T, wait=done)
    if L3_DEBUG:
        dlg = p.dout("dbg_lg", [8, T], F32)
        p.op("gpsimd", lambda e: e.dma_start(out=dlg[:, :], in_=gb.lg[:]),
             inc="so", inc_n=16, wait=done)
    p.wait("gpsimd", "so", p.val("so"))
    return p.build()


class MoEBufs:

    def __init__(self, p, tag, warena_cols=16384):
        self.warena = p.sb(f"{tag}_wa", [128, warena_cols], BF16)
        self.silu_g = [p.sb(f"{tag}_sg{i}", [128, T], BF16) for i in range(4)]
        self.u_bf = [p.sb(f"{tag}_ub{i}", [128, T], BF16) for i in range(4)]
        self.tmp = p.sb(f"{tag}_tmp", [128, T], F32)
        self.cb = p.sb(f"{tag}_cb", [128, T], F32)
        self.mrow = [p.sb(f"{tag}_mr", [1, T], BF16)]
        self.wrow = p.sb(f"{tag}_wr", [1, 8], BF16)
        self.gst = p.sb(f"{tag}_gst", [8, 16], F32)
        self.st8 = p.sb(f"{tag}_st8", [8, 4], F32)
        self.pcb = p.ps(f"{tag}_pcb", [128, TC], F32)

    def wslice(self, i, cols):
        return self.warena[:, i:i + cols]


def emit_wmean(p, mb, gstats_dram, trps):
    p.op("sync", lambda e: e.dma_start(out=mb.gst[:], in_=gstats_dram[:, :]),
         inc="lb2", inc_n=16)
    lv = p.val("lb2")
    p.vop(lambda e: e.reduce_sum(out=mb.st8[:, 0:1], in_=mb.gst[:, 0:8],
                                 axis=mybir.AxisListType.X), wait=[("lb2", lv)])
    p.vop(lambda e: e.reduce_sum(out=mb.st8[:, 1:2], in_=mb.gst[:, 8:16],
                                 axis=mybir.AxisListType.X))
    p.vop(lambda e: e.tensor_scalar_max(out=mb.st8[:, 0:1], in0=mb.st8[:, 0:1],
                                        scalar1=1.0))
    p.vop(lambda e: e.reciprocal(out=mb.st8[:, 2:3], in_=mb.st8[:, 0:1]))
    p.vop(lambda e: e.tensor_tensor(out=mb.st8[:, 3:4], in0=mb.st8[:, 1:2],
                                    in1=mb.st8[:, 2:3], op=ALU.mult))
    dvv = p.val("dv")
    p.op("tensor", lambda e: e.transpose(
        out=trps[0:1, 0:8], in_=mb.st8[0:8, 3:4], identity=mb.ey8_t[0:8, 0:8]),
        inc="pe", wait=[("dv", dvv)])
    pev = p.val("pe")
    p.vop(lambda e: e.tensor_copy(out=mb.wrow[:], in_=trps[0:1, 0:8]),
          wait=[("pe", pev)])
    return p.val("dv")


def emit_moe(p, mb, xn_tiles, acc_tiles, pool, stage, wdram, tok=T):
    nt = ceil_div(tok, TC)
    experts = [(e, 512, 4) for e in range(8)] + [("sh0", 512, 4), ("sh1", 512, 4)]
    down_pe = 0
    prod_dv = 0
    mrow_li = 0
    for eid, width, nf in experts:
        if isinstance(eid, str):
            half = int(eid[2])
            g_dram = [(wdram["sgw"], k * 128, 128, half * 512, width, None)
                      for k in range(8)]
            u_dram = [(wdram["suw"], k * 128, 128, half * 512, width, None)
                      for k in range(8)]
            d_dram = [(wdram["sdw"], half * 512 + k * 128, 128, 0, 1024, None)
                      for k in range(nf)]
        else:
            g_dram = [(wdram["egw"][eid], k * 128, 128, 0, width, None)
                      for k in range(8)]
            u_dram = [(wdram["euw"][eid], k * 128, 128, 0, width, None)
                      for k in range(8)]
            d_dram = [(wdram["edw"][eid], k * 128, 128, 0, 1024, None)
                      for k in range(nf)]
        wg = [mb.wslice(k * width, width) for k in range(8)]
        wu = [mb.wslice(8 * width + k * width, width) for k in range(8)]
        wd = [mb.wslice(16 * width + k * 1024, 1024) for k in range(nf)]
        wlist = []
        for (dr, r0, nr, cc0, c, _), dst in zip(g_dram, wg):
            wlist.append((dr, r0, nr, cc0, c, dst))
        for (dr, r0, nr, cc0, c, _), dst in zip(u_dram, wu):
            wlist.append((dr, r0, nr, cc0, c, dst))
        for (dr, r0, nr, cc0, c, _), dst in zip(d_dram, wd):
            wlist.append((dr, r0, nr, cc0, c, dst))
        w_dv = emit_stream_weights(p, wlist, stage,
                                   start_wait=[("pe", down_pe)])
        w_ready = [("dv", w_dv)]
        if not isinstance(eid, str):
            p.op("sync", lambda e_, _e=eid: e_.dma_start(
                out=mb.mrow[0][:], in_=wdram["m1"][_e:_e + 1, :]),
                inc="li", inc_n=16, wait=[("pe", p.val("pe"))])
            mrow_li = p.val("li")
            for tci, (c0, c1) in enumerate(tchunks(tok)):
                p.op("tensor", lambda e, _e=eid, _c0=c0, _c1=c1: e.matmul(
                    out=mb.pcb[:, 0:_c1 - _c0],
                    lhsT=mb.wrow[0:1, _e:_e + 1].broadcast_to((1, 128)),
                    rhs=mb.mrow[0][0:1, _c0:_c1], start=True, stop=True),
                    inc="pe", wait=[("dv", p.val("dv")), ("li", mrow_li)])
                pev = p.val("pe")
                p.op("vector", lambda e, _c0=c0, _c1=c1: e.tensor_copy(
                    out=mb.cb[:, _c0:_c1], in_=mb.pcb[:, 0:_c1 - _c0]),
                    inc="dv", wait=[("pe", pev)])
        cb_dv = p.val("dv")

        def epi_g(of, tci, c0, c1, ps_ap, pev):
            p.op("scalar", lambda e: e.activation(
                out=mb.silu_g[of][:, c0:c1], in_=ps_ap, func=AF.Silu),
                inc="ac", wait=[("pe", pev), ("dv", prod_dv)])
            return [("ac", p.val("ac"))]

        def epi_u(of, tci, c0, c1, ps_ap, pev):
            p.op("scalar", lambda e: e.activation(
                out=mb.u_bf[of][:, c0:c1], in_=ps_ap, func=AF.Copy),
                inc="ac", wait=[("pe", pev), ("dv", prod_dv)])
            return [("ac", p.val("ac"))]

        emit_proj(p, "g", lambda k, of: wg[k][:, of * 128:(of + 1) * 128],
                  xn_tiles, NKT, nf, tok, epi_g, pool, rhs_ready=w_ready)
        emit_proj(p, "u", lambda k, of: wu[k][:, of * 128:(of + 1) * 128],
                  xn_tiles, NKT, nf, tok, epi_u, pool, rhs_ready=w_ready)
        gu_ac = p.val("ac")
        for f in range(nf):
            if not isinstance(eid, str):
                p.vop(lambda e, _f=f: e.tensor_tensor(
                    out=mb.tmp[:, 0:tok], in0=mb.silu_g[_f][:, 0:tok],
                    in1=mb.u_bf[_f][:, 0:tok], op=ALU.mult),
                    wait=[("ac", gu_ac), ("dv", cb_dv)])
                p.vop(lambda e, _f=f: e.tensor_tensor(
                    out=mb.silu_g[_f][:, 0:tok], in0=mb.tmp[:, 0:tok],
                    in1=mb.cb[:, 0:tok], op=ALU.mult))
            else:
                p.vop(lambda e, _f=f: e.tensor_tensor(
                    out=mb.silu_g[_f][:, 0:tok], in0=mb.silu_g[_f][:, 0:tok],
                    in1=mb.u_bf[_f][:, 0:tok], op=ALU.mult),
                    wait=[("ac", gu_ac)])
        prod_dv = p.val("dv")

        def epi_d(of, tci, c0, c1, ps_ap, pev):
            p.op("vector", lambda e: e.tensor_tensor(
                out=acc_tiles[of][:, c0:c1], in0=acc_tiles[of][:, c0:c1],
                in1=ps_ap, op=ALU.add),
                inc="dv", wait=[("pe", pev)])
            return [("dv", p.val("dv"))]

        emit_proj(p, "d", lambda k, of: wd[k][:, of * 128:(of + 1) * 128],
                  mb.silu_g, nf, NKT, tok, epi_d, pool,
                  rhs_ready=[("dv", prod_dv)])
        down_pe = p.val("pe")
    return p.val("dv")


def build_L4():
    p = Prog("L4")
    x1d = p.din("x1T", [1024, T], F32)
    xn2d = p.din("xn2T", [1024, T], BF16)
    m1d = p.din("m1fm", [8, T], BF16)
    gs1d = p.din("gstats1", [8, 16], F32)
    egw = p.din("egwT", [8, 1024, 512], F32)
    euw = p.din("euwT", [8, 1024, 512], F32)
    edw = p.din("edwT", [8, 512, 1024], F32)
    sgw = p.din("sgwT", [1024, 1024], F32)
    suw = p.din("suwT", [1024, 1024], F32)
    sdw = p.din("sdwT", [1024, 1024], F32)
    wvd = p.din("wvT", [1024, 1024], F32)
    wod = p.din("woT", [1024, 1024], F32)
    gwd = p.din("gwT", [1024, 8], F32)
    lbd = p.din("lb", [8, 1], F32)
    eye8 = p.din("eye8", [8, 8], F32)
    maskd = p.din("mask_row", [1, T], BF16)
    x3o = p.dout("x3T", [1024, T], F32)
    xn4o = p.dout("xn4T", [1024, T], BF16)
    m2o = p.dout("m2_tm", [T, 8], F32)
    gp2o = p.dout("gpart2", [8, 2], F32)
    for sname in ("li", "lw0", "lw1", "lb2", "pe", "dv", "ac", "so"):
        p.sem(sname)
    p.init_consts()

    acc = [p.sb(f"acc{k}", [128, T], F32) for k in range(NKT)]
    xn = [p.sb(f"xn{k}", [128, T], BF16) for k in range(NKT)]
    stage = [p.sb(f"stg{i}", [128, 1024], F32) for i in range(2)]
    mb = MoEBufs(p, "m")
    gwb = [p.sb(f"gwb{k}", [128, 8], BF16) for k in range(NKT)]
    lbcol = p.sb("lbcol", [8, 1], F32)
    ey8 = p.sb("ey8", [8, 8], F32)
    mb.ey8_t = ey8
    ones = p.sb("onesL4", [128, 1], F32)
    onesr_bf = p.sb("onesrbf", [1, 1], BF16)
    v0c = p.sb("v0c", [128, 8], F32)
    v0b = p.sb("v0b", [128, 8], BF16)
    rcol = p.sb("rcol", [128, 8], F32)
    p.op("vector", lambda e: e.memset(ones[:], 1.0), inc="dv")
    p.op("vector", lambda e: e.memset(onesr_bf[:], 1.0), inc="dv")
    gb = GateBufs(p, gwb, lbcol, ey8, ones)
    ln = LNScratch(p, "ln", sq_bufs=[mb.tmp[:, 0:TC], mb.tmp[:, TC:2 * TC]])
    pool = [[p.ps(f"pp{i}", [128, TC], F32), []] for i in range(2)]

    emit_load_fm(p, x1d, acc, 1024, T, sem="li")
    emit_load_fm(p, xn2d, xn, 1024, T, sem="li")
    p.op("sync", lambda e: e.dma_start(out=lbcol[:], in_=lbd[:, :]), inc="lb2", inc_n=16)
    p.op("sync", lambda e: e.dma_start(out=ey8[:], in_=eye8[:, :]), inc="lb2", inc_n=16)
    li_all = p.val("li")
    p.wait("vector", "li", li_all)
    p.wait("tensor", "li", li_all)
    p.wait("scalar", "li", li_all)
    p.wait("tensor", "lb2", p.val("lb2"))

    emit_wmean(p, mb, gs1d, gb.pst)

    wdram = {"egw": egw, "euw": euw, "edw": edw, "sgw": sgw, "suw": suw,
             "sdw": sdw, "m1": m1d}
    emit_moe(p, mb, xn, acc, pool, stage, wdram)

    emit_ln(p, ln, acc, xn, T, [("dv", p.val("dv"))])
    xn3_dv = p.val("dv")

    wvb = [mb.wslice(k * 1024, 1024) for k in range(8)]
    wob = [mb.wslice(8192 + k * 1024, 1024) for k in range(8)]
    wlist = [(wvd, k * 128, 128, 1024, wvb[k]) for k in range(8)]
    wlist += [(wod, k * 128, 128, 1024, wob[k]) for k in range(8)]
    wlist += [(gwd, k * 128, 128, 8, gwb[k]) for k in range(8)]
    wab_dv = emit_stream_weights(p, wlist, stage,
                                 start_wait=[("pe", p.val("pe"))])
    SRC = 1920
    slot = 0
    for of in range(8):
        ps_t, frees = pool[slot % 2]
        for k in range(NKT):
            p.op("tensor", lambda e, _k=k, _of=of, _ps=ps_t: e.matmul(
                out=_ps[:, 0:1], lhsT=wvb[_k][:, _of * 128:(_of + 1) * 128],
                rhs=xn[_k][:, SRC:SRC + 1],
                start=(_k == 0), stop=(_k == NKT - 1)),
                inc="pe" if k == NKT - 1 else None,
                wait=(frees + [("dv", wab_dv), ("dv", xn3_dv)]) if k == 0 else [])
        pev = p.val("pe")
        p.vop(lambda e, _of=of, _ps=ps_t: e.tensor_copy(
            out=v0c[:, _of:_of + 1], in_=_ps[:, 0:1]), wait=[("pe", pev)])
        pool[slot % 2] = [ps_t, [("dv", p.val("dv"))]]
        slot += 1
    p.vop(lambda e: e.tensor_copy(out=v0b[:], in_=v0c[:]))
    v0_dv = p.val("dv")
    for of in range(8):
        ps_t, frees = pool[slot % 2]
        for k in range(NKT):
            p.op("tensor", lambda e, _k=k, _of=of, _ps=ps_t: e.matmul(
                out=_ps[:, 0:1], lhsT=wob[_k][:, _of * 128:(_of + 1) * 128],
                rhs=v0b[:, _k:_k + 1],
                start=(_k == 0), stop=(_k == NKT - 1)),
                inc="pe" if k == NKT - 1 else None,
                wait=(frees + [("dv", v0_dv)]) if k == 0 else [])
        pev = p.val("pe")
        p.vop(lambda e, _of=of, _ps=ps_t: e.tensor_copy(
            out=rcol[:, _of:_of + 1], in_=_ps[:, 0:1]), wait=[("pe", pev)])
        pool[slot % 2] = [ps_t, [("dv", p.val("dv"))]]
        slot += 1
    r_dv = p.val("dv")

    p.op("sync", lambda e: e.dma_start(out=mb.mrow[0][:], in_=maskd[0:1, :]),
         inc="lb2", inc_n=16, wait=[("dv", p.val("dv"))])
    mk_lv = p.val("lb2")
    for tci in range(4):
        c0, c1 = tci * TC, (tci + 1) * TC
        p.op("tensor", lambda e, _c0=c0, _c1=c1: e.matmul(
            out=mb.pcb[:, 0:_c1 - _c0],
            lhsT=mb.wrow[0:1, 0:1].broadcast_to((1, 128)) if False else onesr_bf[0:1, 0:1].broadcast_to((1, 128)),
            rhs=mb.mrow[0][0:1, _c0:_c1], start=True, stop=True),
            inc="pe", wait=[("lb2", mk_lv), ("dv", p.val("dv"))])
        pev = p.val("pe")
        p.op("vector", lambda e, _c0=c0, _c1=c1: e.tensor_copy(
            out=mb.cb[:, _c0:_c1], in_=mb.pcb[:, 0:_c1 - _c0]),
            inc="dv", wait=[("pe", pev)])
    mb_dv = p.val("dv")

    for of in range(8):
        p.vop(lambda e, _of=of: e.tensor_single_scalar(
            out=mb.tmp[:], in_=xn[_of][:], scalar=rcol[:, _of:_of + 1],
            op=ALU.subtract), wait=[("dv", mb_dv), ("dv", r_dv)])
        p.vop(lambda e: e.tensor_tensor(
            out=mb.tmp[:], in0=mb.tmp[:], in1=mb.cb[:], op=ALU.mult))
        p.vop(lambda e, _of=of: e.tensor_tensor(
            out=acc[_of][:], in0=acc[_of][:], in1=xn[_of][:], op=ALU.add))
        p.vop(lambda e, _of=of: e.tensor_tensor(
            out=acc[_of][:], in0=acc[_of][:], in1=mb.tmp[:], op=ALU.subtract))
    x3_dv = p.val("dv")

    emit_ln(p, ln, acc, xn, T, [("dv", x3_dv)])
    emit_gate(p, gb, xn, [("dv", p.val("dv"))], m2o, gp2o)

    done = [("dv", p.val("dv")), ("ac", p.val("ac"))]
    emit_store_fm(p, x3o, acc, 1024, T, wait=done)
    emit_store_fm(p, xn4o, xn, 1024, T, wait=done)
    p.wait("gpsimd", "so", p.val("so"))
    return p.build()


class Carver:

    def __init__(self, p, name, cols):
        self.t = p.sb(name, [128, cols], BF16)
        self.cols = cols
        self.i = 0

    def take(self, cols, dtype=BF16):
        need = cols * (2 if dtype == F32 else 1)
        ap = self.t[:, self.i:self.i + need]
        self.i += need
        assert self.i <= self.cols, f"carver overflow {self.i}>{self.cols}"
        if dtype == F32:
            ap = ap.bitcast(F32)
        return ap

    def reset(self):
        self.i = 0


class NS:
    def __init__(self, **kw):
        self.__dict__.update(kw)

    def wslice(self, i, cols):
        return self.warena[:, i:i + cols]


TOKH = T + 3


def emit_mamba_pre(p, h_tiles, h_ready, wd, outs, pool, stage, arena, tokh=TOKH):
    tok = tokh - 3
    ntc = ceil_div(tokh, TC)
    arena.reset()
    uraw = arena.take(tokh + 1, F32)
    cvsc = arena.take(tok, F32)
    uf32 = arena.take(tok, F32)
    ubf = arena.take(tok, BF16)
    xdbl = arena.take(tok, F32)
    xdblb = arena.take(tok, BF16)
    dtt = [arena.take(tok, F32) for _ in range(2)]
    spA = arena.take(TC, F32)
    spB = arena.take(TC, F32)
    wz = [NS(t=None) for _ in range(8)]

    winT, wxT, wdtT = wd["winT"], wd["wxT"], wd["wdtT"]
    convw, convb, bdt = wd["convw"], wd["convb"], wd["bdt"]
    u_out, dt_out, bc_out, z_out = outs["u"], outs["dt"], outs["bc"], outs["z"]

    wxb = [wd["mb"].wslice(k * 96, 96) for k in range(16)]
    wdtb = wd["mb"].wslice(16 * 96, 2048)
    winb = [wd["mb"].wslice(16 * 96 + 2048 + k * 512, 512) for k in range(8)]
    cwcols = wd["cwcols"]
    cbcols = wd["cbcols"]
    bdtcols = wd["bdtcols"]

    wlist = [(wxT, k * 128, 128, 0, 96, wxb[k]) for k in range(16)]
    wlist += [(wdtT, 0, 64, 0, 1024, wdtb[:, 0:1024]),
              (wdtT, 0, 64, 1024, 1024, wdtb[:, 1024:2048])]
    w_dv = emit_stream_weights(p, wlist, stage, start_wait=[("pe", p.val("pe"))])

    pxd = p.ps("pxd", [96, TC], F32) if not hasattr(p, "_pxd") else p._pxd
    p._pxd = pxd
    xdbl_dv = 0
    for og in range(8):
        wlist = [(winT, k * 128, 128, og * 512, 512, winb[k]) for k in range(8)]
        wg_dv = emit_stream_weights(p, wlist, stage,
                                    start_wait=[("pe", p.val("pe"))])
        for ol in range(4):
            of = og * 4 + ol
            is_u = of < 16
            slot = 0
            for tci, (c0, c1) in enumerate(tchunks(tokh)):
                ps_t, frees = pool[slot % 2]
                slot += 1
                for k in range(NKT):
                    p.op("tensor", lambda e, _k=k, _ol=ol, _c0=c0, _c1=c1, _ps=ps_t: e.matmul(
                        out=_ps[:, 0:_c1 - _c0],
                        lhsT=winb[_k][:, _ol * 128:(_ol + 1) * 128],
                        rhs=h_tiles[_k][:, _c0:_c1],
                        start=(_k == 0), stop=(_k == NKT - 1)),
                        inc="pe" if k == NKT - 1 else None,
                        wait=(frees + [("dv", wg_dv)] + h_ready) if k == 0 else [])
                pev = p.val("pe")
                p.op("vector", lambda e, _c0=c0, _c1=c1, _ps=ps_t: e.tensor_copy(
                    out=uraw[:, _c0:_c1], in_=_ps[:, 0:_c1 - _c0]),
                    inc="dv", wait=[("pe", pev)])
                pool[(slot - 1) % 2] = [ps_t, [("dv", p.val("dv"))]]
            xz_dv = p.val("dv")
            if not is_u:
                p.op("gpsimd", lambda e, _of=of: e.dma_start(
                    out=z_out[(_of - 16) * 128:(_of - 15) * 128, :],
                    in_=uraw[:, 3:tokh]), inc="so", inc_n=16,
                    wait=[("dv", xz_dv)])
                p.wait("vector", "so", p.val("so"))
                continue
            p.vop(lambda e, _of=of: e.tensor_single_scalar(
                out=cvsc[:, 0:tok], in_=uraw[:, 0:tok],
                scalar=cwcols[:, _of * 4:_of * 4 + 1], op=ALU.mult))
            for r in range(1, 4):
                p.vop(lambda e, _of=of, _r=r: e.scalar_tensor_tensor(
                    out=cvsc[:, 0:tok], in0=uraw[:, _r:_r + tok],
                    scalar=cwcols[:, _of * 4 + _r:_of * 4 + _r + 1],
                    in1=cvsc[:, 0:tok], op0=ALU.mult, op1=ALU.add))
            cv_dv = p.val("dv")
            p.op("scalar", lambda e, _of=of: e.activation(
                out=uf32[:, 0:tok], in_=cvsc[:, 0:tok], func=AF.Silu,
                bias=cbcols[:, _of:_of + 1]),
                inc="ac", wait=[("dv", cv_dv)])
            acv = p.val("ac")
            p.op("vector", lambda e: e.tensor_copy(
                out=ubf[:, 0:tok], in_=uf32[:, 0:tok]),
                inc="dv", wait=[("ac", acv)])
            u_dv = p.val("dv")
            p.op("gpsimd", lambda e, _of=of: e.dma_start(
                out=u_out[_of * 128:(_of + 1) * 128, :], in_=uf32[:, 0:tok]),
                inc="so", inc_n=16, wait=[("dv", u_dv)])
            u_so = p.val("so")
            for tci in range(4):
                c0, c1 = tci * TC, (tci + 1) * TC
                p.op("tensor", lambda e, _of=of, _c0=c0, _c1=c1: e.matmul(
                    out=pxd[:, 0:_c1 - _c0], lhsT=wxb[_of][:, 0:96],
                    rhs=ubf[:, _c0:_c1], start=True, stop=True),
                    inc="pe",
                    wait=[("dv", max(w_dv, u_dv, xdbl_dv))])
                pev = p.val("pe")
                if of == 0:
                    p.op("vector", lambda e, _c0=c0, _c1=c1: e.tensor_copy(
                        out=xdbl[0:96, _c0:_c1], in_=pxd[0:96, 0:_c1 - _c0]),
                        inc="dv", wait=[("pe", pev)])
                else:
                    p.op("vector", lambda e, _c0=c0, _c1=c1: e.tensor_tensor(
                        out=xdbl[0:96, _c0:_c1], in0=xdbl[0:96, _c0:_c1],
                        in1=pxd[0:96, 0:_c1 - _c0], op=ALU.add),
                        inc="dv", wait=[("pe", pev)])
                xdbl_dv = p.val("dv")
            p.wait("vector", "so", u_so)
            p.wait("scalar", "so", u_so)
    p.vop(lambda e: e.tensor_copy(out=xdblb[0:64, 0:tok], in_=xdbl[0:64, 0:tok]))
    xb_dv = p.val("dv")
    dt_dma_so = 0
    for of in range(16):
        dtile = dtt[of % 2]
        for tci in range(4):
            c0, c1 = tci * TC, (tci + 1) * TC
            ps_t, frees = pool[tci % 2]
            p.op("tensor", lambda e, _of=of, _c0=c0, _c1=c1, _ps=ps_t: e.matmul(
                out=_ps[:, 0:_c1 - _c0],
                lhsT=wdtb[0:64, _of * 128:(_of + 1) * 128],
                rhs=xdblb[0:64, _c0:_c1], start=True, stop=True),
                inc="pe", wait=(frees + [("dv", xb_dv)]))
            pev = p.val("pe")
            p.vop(lambda e, _of=of, _ps=ps_t, _w=c1 - c0: e.tensor_scalar_add(
                out=spA[:, 0:_w], in0=_ps[:, 0:_w],
                scalar1=bdtcols[:, _of:_of + 1]), wait=[("pe", pev)])
            pool[tci % 2] = [ps_t, [("dv", p.val("dv"))]]
            p.vop(lambda e, _w=c1 - c0: e.tensor_scalar_mul(
                out=spB[:, 0:_w], in0=spA[:, 0:_w], scalar1=-1.0))
            p.vop(lambda e, _w=c1 - c0: e.tensor_tensor(
                out=spB[:, 0:_w], in0=spA[:, 0:_w], in1=spB[:, 0:_w],
                op=ALU.min))
            dvv = p.val("dv")
            p.op("scalar", lambda e, _w=c1 - c0: e.activation(
                out=spB[:, 0:_w], in_=spB[:, 0:_w], func=AF.Exp),
                inc="ac", wait=[("dv", dvv)])
            p.op("scalar", lambda e, _w=c1 - c0: e.activation(
                out=spB[:, 0:_w], in_=spB[:, 0:_w], func=AF.Ln, bias=1.0),
                inc="ac")
            acv = p.val("ac")
            p.vop(lambda e, _c0=c0, _w=c1 - c0, _d=dtile: e.scalar_tensor_tensor(
                out=_d[:, _c0:_c0 + _w], in0=spA[:, 0:_w], scalar=0.0,
                in1=spB[:, 0:_w], op0=ALU.max, op1=ALU.add),
                wait=[("ac", acv), ("so", dt_dma_so)])
        dt_dv = p.val("dv")
        p.op("gpsimd", lambda e, _of=of, _d=dtile: e.dma_start(
            out=dt_out[_of * 128:(_of + 1) * 128, :], in_=_d[:, 0:tok]),
            inc="so", inc_n=16, wait=[("dv", dt_dv)])
        if of % 2 == 1:
            dt_dma_so = p.val("so") - 16
    p.op("gpsimd", lambda e: e.dma_start(
        out=bc_out[:, :], in_=xdbl[64:96, 0:tok]), inc="so", inc_n=16,
        wait=[("dv", xdbl_dv), ("dv", p.val("dv"))])


def build_L5():
    p = Prog("L5")
    x3d = p.din("x3T", [1024, TOKH], F32)
    xn4d = p.din("xn4T", [1024, TOKH], BF16)
    m2d = p.din("m2fm", [8, TOKH], BF16)
    gs2d = p.din("gstats2", [8, 16], F32)
    egw = p.din("egwT", [8, 1024, 512], F32)
    euw = p.din("euwT", [8, 1024, 512], F32)
    edw = p.din("edwT", [8, 512, 1024], F32)
    sgw = p.din("sgwT", [1024, 1024], F32)
    suw = p.din("suwT", [1024, 1024], F32)
    sdw = p.din("sdwT", [1024, 1024], F32)
    winT = p.din("winT", [1024, 4096], F32)
    wxT = p.din("wxT", [2048, 96], F32)
    wdtT = p.din("wdtT", [64, 2048], F32)
    convw = p.din("convw", [128, 64], F32)
    convb = p.din("convb", [128, 16], F32)
    bdt = p.din("bdt", [128, 16], F32)
    eye8 = p.din("eye8", [8, 8], F32)
    x4o = p.dout("x4T", [1024, T], F32)
    u_o = p.dout("u", [2048, T], F32)
    dt_o = p.dout("dt", [2048, T], F32)
    bc_o = p.dout("bc", [32, T], F32)
    z_o = p.dout("z", [2048, T], F32)
    for sname in ("li", "lw0", "lw1", "lb2", "pe", "dv", "ac", "so"):
        p.sem(sname)
    p.init_consts()

    acc = [p.sb(f"acc{k}", [128, TOKH], F32) for k in range(NKT)]
    xn = [p.sb(f"xn{k}", [128, TOKH], BF16) for k in range(NKT)]
    stage = [p.sb(f"stg{i}", [128, 1024], F32) for i in range(2)]
    arena = Carver(p, "arena", 30760)
    warena = p.sb("warena", [128, 16384], BF16)
    cwcols = p.sb("cwcols", [128, 64], F32)
    cbcols = p.sb("cbcols", [128, 16], F32)
    bdtcols = p.sb("bdtcols", [128, 16], F32)
    ey8 = p.sb("ey8", [8, 8], F32)
    ones = p.sb("onesL5", [128, 1], F32)
    p.op("vector", lambda e: e.memset(ones[:], 1.0), inc="dv")

    arena.reset()
    mb = NS(warena=warena,
            silu_g=[arena.take(TOKH + 1) for _ in range(4)],
            u_bf=[arena.take(TOKH + 1) for _ in range(4)],
            tmp=arena.take(TOKH + 1, F32),
            cb=arena.take(TOKH + 1, F32),
            mrow=[p.sb("m_mr", [1, TOKH], BF16)],
            wrow=p.sb("m_wr", [1, 8], BF16),
            gst=p.sb("m_gst", [8, 16], F32),
            st8=p.sb("m_st8", [8, 4], F32),
            pcb=p.ps("m_pcb", [128, TC], F32),
            ey8_t=ey8)
    ln = LNScratch(p, "ln", sq_bufs=[mb.tmp[:, 0:TC], mb.tmp[:, TC:2 * TC]],
                   rows_buf=arena.t[0:1, 25000:25000 + 4 * TC * 2].bitcast(F32),
                   absb_buf=arena.t[:, 25000 + 4 * TC * 2:25000 + 5 * TC * 2].bitcast(F32))
    ptr8 = p.ps("ptr8", [128, 8], F32)
    pool = [[p.ps(f"pp{i}", [128, TC], F32), []] for i in range(2)]

    emit_load_fm(p, x3d, acc, 1024, TOKH, sem="li")
    emit_load_fm(p, xn4d, xn, 1024, TOKH, sem="li")
    p.op("sync", lambda e: e.dma_start(out=ey8[:], in_=eye8[:, :]), inc="lb2", inc_n=16)
    p.op("sync", lambda e: e.dma_start(out=cwcols[:], in_=convw[:, :]), inc="lb2", inc_n=16)
    p.op("sync", lambda e: e.dma_start(out=cbcols[:], in_=convb[:, :]), inc="lb2", inc_n=16)
    p.op("sync", lambda e: e.dma_start(out=bdtcols[:], in_=bdt[:, :]), inc="lb2", inc_n=16)
    li_all = p.val("li")
    for eng in ("vector", "tensor", "scalar"):
        p.wait(eng, "li", li_all)
        p.wait(eng, "lb2", p.val("lb2"))

    emit_wmean(p, mb, gs2d, ptr8)
    wdram = {"egw": egw, "euw": euw, "edw": edw, "sgw": sgw, "suw": suw,
             "sdw": sdw, "m1": m2d}
    emit_moe(p, mb, xn, acc, pool, stage, wdram, tok=TOKH)

    x4_dv = p.val("dv")
    for k in range(NKT):
        p.op("gpsimd", lambda e, _k=k: e.dma_start(
            out=x4o[_k * 128:(_k + 1) * 128, :], in_=acc[_k][:, 3:TOKH]),
            inc="so", inc_n=16, wait=[("dv", x4_dv)])

    emit_ln(p, ln, acc, xn, TOKH, [("dv", p.val("dv"))])

    wd = {"winT": winT, "wxT": wxT, "wdtT": wdtT, "convw": convw,
          "convb": convb, "bdt": bdt, "mb": mb, "cwcols": cwcols,
          "cbcols": cbcols, "bdtcols": bdtcols}
    outs = {"u": u_o, "dt": dt_o, "bc": bc_o, "z": z_o}
    emit_mamba_pre(p, xn, [("dv", p.val("dv"))], wd, outs, pool, stage, arena)

    p.wait("gpsimd", "so", p.val("so"))
    return p.build()


def build_L6():
    p = Prog("L6")
    S = 4096
    dtd = p.din("dt_dh", [1024, S], F32)
    ud = p.din("u_dh", [1024, S], F32)
    brep = p.din("Brep", [2048, S], F32)
    crep = p.din("Crep", [2048, S], F32)
    alogd = p.din("alog", [1024, 16], F32)
    dd = p.din("dcol", [128, 8], F32)
    yo = p.dout("y", [1024, S], F32)
    for sname in ("ld", "lb", "pe", "dv", "ac", "so"):
        p.sem(sname)
    p.init_consts()

    dt_t = p.sb("dt_t", [128, S], F32)
    u_t = p.sb("u_t", [128, S], F32)
    dtu = p.sb("dtu", [128, S], F32)
    dA = p.sb("dA", [128, S], F32)
    dBu = p.sb("dBu", [128, S], F32)
    hh = p.sb("hh", [128, S], F32)
    yy = p.sb("yy", [128, S], F32)
    bb = p.sb("bb", [128, S], F32)
    cc = p.sb("cc", [128, S], F32)
    at = [p.sb(f"at{i}", [128, 16], F32) for i in range(8)]
    dcol = p.sb("dcolt", [128, 8], F32)

    for i in range(8):
        p.op("sync", lambda e, _i=i: e.dma_start(
            out=at[_i][:], in_=alogd[_i * 128:(_i + 1) * 128, :]),
            inc="lb", inc_n=16)
    p.op("sync", lambda e: e.dma_start(out=dcol[:], in_=dd[:, :]), inc="lb", inc_n=16)
    lbv = p.val("lb")
    p.wait("scalar", "lb", lbv)
    for i in range(8):
        p.op("scalar", lambda e, _i=i: e.activation(
            out=at[_i][:], in_=at[_i][:], func=AF.Exp), inc="ac")
    acv = p.val("ac")
    for i in range(8):
        p.vop(lambda e, _i=i: e.tensor_scalar_mul(
            out=at[_i][:], in0=at[_i][:], scalar1=-1.0),
            wait=[("ac", acv)] if i == 0 else [])
    a_dv = p.val("dv")

    nld = 0
    nlb = lbv
    for dt_i in range(8):
        prev_store = [("so", p.val("so"))]
        p.op("sync", lambda e, _i=dt_i: e.dma_start(
            out=dt_t[:], in_=dtd[_i * 128:(_i + 1) * 128, :]),
            inc="ld", inc_n=16, wait=prev_store)
        p.op("sync", lambda e, _i=dt_i: e.dma_start(
            out=u_t[:], in_=ud[_i * 128:(_i + 1) * 128, :]),
            inc="ld", inc_n=16)
        nld = p.val("ld")
        p.vop(lambda e: e.tensor_tensor(
            out=dtu[:], in0=dt_t[:], in1=u_t[:], op=ALU.mult),
            wait=[("ld", nld), ("dv", a_dv)])
        dtu_dv = p.val("dv")
        for n in range(16):
            bc_guard = [("dv", p.val("dv"))]
            p.op("gpsimd", lambda e, _n=n: e.dma_start(
                out=bb[:], in_=brep[_n * 128:(_n + 1) * 128, :]),
                inc="lb", inc_n=16, wait=bc_guard)
            p.op("gpsimd", lambda e, _n=n: e.dma_start(
                out=cc[:], in_=crep[_n * 128:(_n + 1) * 128, :]),
                inc="lb", inc_n=16)
            nlb = p.val("lb")
            p.op("scalar", lambda e, _n=n, _i=dt_i: e.activation(
                out=dA[:], in_=dt_t[:], func=AF.Exp,
                scale=at[_i][:, _n:_n + 1]),
                inc="ac", wait=[("dv", p.val("dv")), ("ld", nld)])
            acv = p.val("ac")
            p.vop(lambda e: e.tensor_tensor(
                out=dBu[:], in0=dtu[:], in1=bb[:], op=ALU.mult),
                wait=[("lb", nlb)])
            p.vop(lambda e: e.tensor_tensor_scan(
                out=hh[:], data0=dA[:], data1=dBu[:], initial=0.0,
                op0=ALU.mult, op1=ALU.add),
                wait=[("ac", acv)])
            p.vop(lambda e: e.tensor_tensor(
                out=dA[:], in0=hh[:], in1=cc[:], op=ALU.mult))
            if n == 0:
                p.vop(lambda e: e.tensor_copy(out=yy[:], in_=dA[:]))
            else:
                p.vop(lambda e: e.tensor_tensor(
                    out=yy[:], in0=yy[:], in1=dA[:], op=ALU.add))
        p.vop(lambda e, _i=dt_i: e.scalar_tensor_tensor(
            out=yy[:], in0=u_t[:], scalar=dcol[:, _i:_i + 1], in1=yy[:],
            op0=ALU.mult, op1=ALU.add))
        y_dv = p.val("dv")
        p.op("gpsimd", lambda e, _i=dt_i: e.dma_start(
            out=yo[_i * 128:(_i + 1) * 128, :], in_=yy[:]),
            inc="so", inc_n=16, wait=[("dv", y_dv)])
    p.wait("gpsimd", "so", p.val("so"))
    return p.build()


def build_L7():
    p = Prog("L7")
    yd = p.din("yT", [2048, TOKH], F32)
    zd = p.din("zT", [2048, TOKH], F32)
    woutT = p.din("woutT", [2048, 1024], F32)
    winT = p.din("winT", [1024, 4096], F32)
    wxT = p.din("wxT", [2048, 96], F32)
    wdtT = p.din("wdtT", [64, 2048], F32)
    convw = p.din("convw", [128, 64], F32)
    convb = p.din("convb", [128, 16], F32)
    bdt = p.din("bdt", [128, 16], F32)
    u_o = p.dout("u", [2048, T], F32)
    dt_o = p.dout("dt", [2048, T], F32)
    bc_o = p.dout("bc", [32, T], F32)
    z_o = p.dout("z", [2048, T], F32)
    for sname in ("li", "lw0", "lw1", "lb2", "pe", "dv", "ac", "so"):
        p.sem(sname)
    p.init_consts()

    arena = Carver(p, "arena", 32920)
    warena = p.sb("warenaL7", [128, 16384], BF16)
    stage = [p.sb(f"stg{i}", [128, 1024], F32) for i in range(2)]
    y_t = p.sb("y_t", [128, TOKH], F32)
    z_t = p.sb("z_t", [128, TOKH], F32)
    h1 = [p.sb(f"h1_{k}", [128, TOKH], BF16) for k in range(NKT)]
    cwcols = p.sb("cwcols", [128, 64], F32)
    cbcols = p.sb("cbcols", [128, 16], F32)
    bdtcols = p.sb("bdtcols", [128, 16], F32)
    mb = NS(warena=warena)
    pool = [[p.ps(f"pp{i}", [128, TC], F32), []] for i in range(2)]

    p.op("sync", lambda e: e.dma_start(out=cwcols[:], in_=convw[:, :]), inc="lb2", inc_n=16)
    p.op("sync", lambda e: e.dma_start(out=cbcols[:], in_=convb[:, :]), inc="lb2", inc_n=16)
    p.op("sync", lambda e: e.dma_start(out=bdtcols[:], in_=bdt[:, :]), inc="lb2", inc_n=16)
    for eng in ("vector", "tensor", "scalar"):
        p.wait(eng, "lb2", p.val("lb2"))

    woutb = [warena[:, k * 1024:(k + 1) * 1024] for k in range(16)]
    wlist = [(woutT, k * 128, 128, 0, 1024, woutb[k]) for k in range(16)]
    wo_dv = emit_stream_weights(p, wlist, stage)

    arena.reset()
    ygb = [arena.take(TOKH + 1) for _ in range(16)]
    for k in range(16):
        guard = [("dv", p.val("dv"))]
        p.op("sync", lambda e, _k=k: e.dma_start(
            out=y_t[:], in_=yd[_k * 128:(_k + 1) * 128, :]),
            inc="li", inc_n=16, wait=guard)
        p.op("sync", lambda e, _k=k: e.dma_start(
            out=z_t[:], in_=zd[_k * 128:(_k + 1) * 128, :]),
            inc="li", inc_n=16)
        liv = p.val("li")
        p.op("scalar", lambda e: e.activation(
            out=z_t[:], in_=z_t[:], func=AF.Silu), inc="ac",
            wait=[("li", liv)])
        acv = p.val("ac")
        p.vop(lambda e, _k=k: e.tensor_tensor(
            out=ygb[_k][:, 0:TOKH], in0=y_t[:], in1=z_t[:], op=ALU.mult),
            wait=[("ac", acv)])
    yg_dv = p.val("dv")

    def epi_h1(of, tci, c0, c1, ps_ap, pev):
        p.op("vector", lambda e: e.tensor_copy(
            out=h1[of][:, c0:c1], in_=ps_ap), inc="dv", wait=[("pe", pev)])
        return [("dv", p.val("dv"))]

    emit_proj(p, "h1", lambda k, of: woutb[k][:, of * 128:(of + 1) * 128],
              lambda k, c0, c1: ygb[k][:, c0:c1], 16, NKT, TOKH, epi_h1, pool,
              rhs_ready=[("dv", yg_dv), ("dv", wo_dv)])
    h1_dv = p.val("dv")

    wd = {"winT": winT, "wxT": wxT, "wdtT": wdtT, "convw": convw,
          "convb": convb, "bdt": bdt, "mb": mb, "cwcols": cwcols,
          "cbcols": cbcols, "bdtcols": bdtcols}
    outs = {"u": u_o, "dt": dt_o, "bc": bc_o, "z": z_o}
    emit_mamba_pre(p, h1, [("dv", h1_dv)], wd, outs, pool, stage, arena)

    p.wait("gpsimd", "so", p.val("so"))
    return p.build()


def build_L9():
    p = Prog("L9")
    y2d = p.din("y2T", [2048, T], F32)
    z2d = p.din("z2T", [2048, T], F32)
    x4d = p.din("x4T", [1024, T], F32)
    wout1 = p.din("wout1T", [2048, 1024], F32)
    win = p.din("ffn_win", [1024, 64], F32)
    binb = p.din("ffn_bin", [64, 1], F32)
    w1 = p.din("ffn_w1", [64, 16], F32)
    b1 = p.din("ffn_b1", [64, 16], F32)
    w2 = p.din("ffn_w2", [64, 16], F32)
    b2 = p.din("ffn_b2", [64, 1], F32)
    wout = p.din("ffn_wout", [64, 1024], F32)
    bout = p.din("ffn_bout", [128, 8], F32)
    wq = p.din("wqT", [1024, 1024], F32)
    wk = p.din("wkT", [1024, 1024], F32)
    wv = p.din("wvT", [1024, 1024], F32)
    wa = p.din("waT", [1024, 1024], F32)
    war = p.din("waRT", [1024, 1024], F32)
    ba = p.din("ba", [128, 8], F32)
    bar = p.din("baR", [128, 8], F32)
    wb = p.din("wbT", [1024, 16], F32)
    wg = p.din("wgT", [1024, 16], F32)
    bb = p.din("bb", [16, 1], F32)
    bg = p.din("bg", [16, 1], F32)
    lnw = p.din("ln_w", [128, 8], F32)
    lnb = p.din("ln_b", [128, 8], F32)
    selnd = p.din("sel_n", [128, 2], F32)
    selbd = p.din("sel_b", [2, 128], F32)
    x6o = p.dout("x6T", [1024, T], F32)
    kno = p.dout("kn", [1024, T], F32)
    vo = p.dout("v", [1024, T], F32)
    qo = p.dout("q", [1024, T], BF16)
    lao = p.dout("la", [1024, T], F32)
    laro = p.dout("laR", [1024, T], F32)
    bgo = p.dout("betagamma", [32, T], F32)
    for sname in ("li", "lw0", "lw1", "lb2", "pe", "dv", "ac", "so"):
        p.sem(sname)
    p.init_consts()

    acc = [p.sb(f"acc{k}", [128, T], F32) for k in range(NKT)]
    xn = [p.sb(f"xn{k}", [128, T], BF16) for k in range(NKT)]
    stage = [p.sb(f"stg{i}", [128, 1024], F32) for i in range(2)]
    warena = p.sb("warenaL9", [128, 16384], BF16)
    AR = p.sb("arenaL9", [128, 33856], BF16)

    def carve(c0, cols, dtype=BF16, parts=128):
        ap = AR[0:parts, c0:c0 + cols * (2 if dtype == F32 else 1)]
        return ap.bitcast(F32) if dtype == F32 else ap

    TH = T // 2
    ygb = [carve(k * (TH + 4), TH + 4) for k in range(16)]
    y_t = carve(16448, TH, F32)
    z_t = carve(18496, TH, F32)
    sg = carve(0, T, F32)
    hsc = carve(4096, T, F32)
    gacc = carve(8192, T, F32)
    gbf = carve(12288, T)
    scrA = carve(14336, TC, F32)
    scr0 = carve(0, TC, F32)
    scr1 = carve(1024, TC, F32)
    rotf = [carve(2048, T, F32), carve(6144, T, F32)]
    rotb = [carve(10240, T), carve(12288, T)]
    sqt = carve(14336, T, F32)
    nnb = carve(18432, T, F32)
    bgsb = carve(22528, T, F32)
    ln = LNScratch(p, "ln",
                   sq_bufs=[carve(26624, TC, F32), carve(27648, TC, F32)],
                   rows_buf=carve(28672, 4 * TC, F32)[0:1, :],
                   absb_buf=carve(32768, TC, F32))

    smalls = {}
    for nm, shp, dr in (("binb", [64, 1], binb), ("w1", [64, 16], w1),
                        ("b1", [64, 16], b1), ("w2", [64, 16], w2),
                        ("b2", [64, 1], b2), ("bout", [128, 8], bout),
                        ("ba", [128, 8], ba), ("bar", [128, 8], bar),
                        ("bb", [16, 1], bb), ("bg", [16, 1], bg),
                        ("lnw", [128, 8], lnw), ("lnb", [128, 8], lnb)):
        smalls[nm] = p.sb("s_" + nm, shp, F32)
        p.op("sync", lambda e, _d=dr, _n=nm: e.dma_start(
            out=smalls[_n][:], in_=_d[:, :]), inc="lb2", inc_n=16)
    sel_n = p.sb("sel_n_sb", [128, 2], F32)
    sel_b = p.sb("sel_b_sb", [2, 128], F32)
    p.op("sync", lambda e: e.dma_start(out=sel_n[:], in_=selnd[:, :]), inc="lb2", inc_n=16)
    p.op("sync", lambda e: e.dma_start(out=sel_b[:], in_=selbd[:, :]), inc="lb2", inc_n=16)
    for eng in ("vector", "tensor", "scalar"):
        p.wait(eng, "lb2", p.val("lb2"))
    sel_dv = 0
    pool = [[p.ps(f"pp{i}", [128, TC], F32), []] for i in range(2)]
    pn2 = p.ps("pn2", [2, TC], F32)
    pbc2 = p.ps("pbc2", [128, TC], F32)

    emit_load_fm(p, x4d, acc, 1024, T, sem="li")
    x4_li = p.val("li")
    emit_ln(p, ln, acc, acc, T, [("li", x4_li)], residual=True)

    woutb = [warena[:, k * 1024:(k + 1) * 1024] for k in range(16)]
    wlist = [(wout1, k * 128, 128, 0, 1024, woutb[k]) for k in range(16)]
    wo_dv = emit_stream_weights(p, wlist, stage)

    for hf in range(2):
        t0 = hf * TH
        for k in range(16):
            guard = [("dv", p.val("dv"))]
            p.op("sync", lambda e, _k=k, _t0=t0: e.dma_start(
                out=y_t[:], in_=y2d[_k * 128:(_k + 1) * 128, _t0:_t0 + TH]),
                inc="li", inc_n=16, wait=guard)
            p.op("sync", lambda e, _k=k, _t0=t0: e.dma_start(
                out=z_t[:], in_=z2d[_k * 128:(_k + 1) * 128, _t0:_t0 + TH]),
                inc="li", inc_n=16)
            liv = p.val("li")
            p.op("scalar", lambda e: e.activation(
                out=z_t[:], in_=z_t[:], func=AF.Silu), inc="ac",
                wait=[("li", liv)])
            acv = p.val("ac")
            p.vop(lambda e, _k=k: e.tensor_tensor(
                out=ygb[_k][:, 0:TH], in0=y_t[:], in1=z_t[:], op=ALU.mult),
                wait=[("ac", acv)])
        yg_dv = p.val("dv")

        def epi_h2(of, tci, c0, c1, ps_ap, pev, _t0=t0):
            p.op("vector", lambda e: e.tensor_tensor(
                out=acc[of][:, _t0 + c0:_t0 + c1],
                in0=acc[of][:, _t0 + c0:_t0 + c1], in1=ps_ap,
                op=ALU.add), inc="dv", wait=[("pe", pev)])
            return [("dv", p.val("dv"))]

        emit_proj(p, "h2", lambda k, of: woutb[k][:, of * 128:(of + 1) * 128],
                  lambda k, c0, c1: ygb[k][:, c0:c1], 16, NKT, TH, epi_h2,
                  pool, rhs_ready=[("dv", yg_dv), ("dv", wo_dv)])
    x5_dv = p.val("dv")

    emit_ln(p, ln, acc, xn, T, [("dv", x5_dv)])
    xn6_dv = p.val("dv")

    winb = [warena[:, k * 64:(k + 1) * 64] for k in range(8)]
    woutffn = warena[:, 1024:2048]
    wlist = [(win, k * 128, 128, 0, 64, winb[k]) for k in range(8)]
    wlist += [(wout, 0, 64, 0, 1024, woutffn)]
    wffn_dv = emit_stream_weights(p, wlist, stage,
                                  start_wait=[("pe", p.val("pe"))])
    for (c0, c1) in tchunks(T):
        ps_t, frees = pool[0]
        for k in range(NKT):
            p.op("tensor", lambda e, _k=k, _c0=c0, _c1=c1, _ps=ps_t: e.matmul(
                out=_ps[0:64, 0:_c1 - _c0], lhsT=winb[_k][:],
                rhs=xn[_k][:, _c0:_c1], start=(_k == 0), stop=(_k == NKT - 1)),
                inc="pe" if k == NKT - 1 else None,
                wait=(frees + [("dv", wffn_dv), ("dv", xn6_dv)]) if k == 0 else [])
        pev = p.val("pe")
        p.op("vector", lambda e, _c0=c0, _c1=c1, _ps=ps_t: e.tensor_scalar_add(
            out=sg[0:64, _c0:_c1], in0=_ps[0:64, 0:_c1 - _c0],
            scalar1=smalls["binb"][:]), inc="dv", wait=[("pe", pev)])
        pool[0] = [ps_t, [("dv", p.val("dv"))]]
    for i in range(16):
        p.op("scalar", lambda e, _i=i: e.activation(
            out=hsc[0:64, 0:T], in_=sg[0:64, 0:T], func=AF.Relu,
            scale=smalls["w1"][:, _i:_i + 1], bias=smalls["b1"][:, _i:_i + 1]),
            inc="ac", wait=[("dv", p.val("dv"))])
        acv = p.val("ac")
        if i == 0:
            p.vop(lambda e: e.tensor_single_scalar(
                out=gacc[0:64, 0:T], in_=hsc[0:64, 0:T],
                scalar=smalls["w2"][:, 0:1], op=ALU.mult), wait=[("ac", acv)])
        else:
            p.vop(lambda e, _i=i: e.scalar_tensor_tensor(
                out=gacc[0:64, 0:T], in0=hsc[0:64, 0:T],
                scalar=smalls["w2"][:, _i:_i + 1], in1=gacc[0:64, 0:T],
                op0=ALU.mult, op1=ALU.add), wait=[("ac", acv)])
    p.vop(lambda e: e.tensor_scalar_add(
        out=gacc[0:64, 0:T], in0=gacc[0:64, 0:T], scalar1=smalls["b2"][:]))
    p.vop(lambda e: e.tensor_copy(out=gbf[0:64, 0:T], in_=gacc[0:64, 0:T]))
    g_dv = p.val("dv")

    def epi_ffo(of, tci, c0, c1, ps_ap, pev):
        w = c1 - c0
        p.vop(lambda e: e.tensor_scalar_add(
            out=scrA[:, 0:w], in0=ps_ap, scalar1=smalls["bout"][:, of:of + 1]),
            wait=[("pe", pev)])
        p.op("vector", lambda e: e.tensor_tensor(
            out=acc[of][:, c0:c1], in0=acc[of][:, c0:c1], in1=scrA[:, 0:w],
            op=ALU.add), inc="dv")
        return [("dv", p.val("dv"))]

    emit_proj(p, "ffo", lambda k, of: woutffn[0:64, of * 128:(of + 1) * 128],
              lambda k, c0, c1: gbf[0:64, c0:c1], 1, NKT, T, epi_ffo, pool,
              rhs_ready=[("dv", g_dv)])
    x6_dv = p.val("dv")
    emit_store_fm(p, x6o, acc, 1024, T, wait=[("dv", x6_dv)])

    lnw_cols = [smalls["lnw"][:, k:k + 1] for k in range(8)]
    lnb_cols = [smalls["lnb"][:, k:k + 1] for k in range(8)]
    emit_ln(p, ln, acc, xn, T, [("dv", x6_dv)], gain_col=lnw_cols,
            bias_col=lnb_cols)
    xn7_dv = p.val("dv")

    wbands = [warena[:, 8192 + k * 1024:8192 + (k + 1) * 1024] for k in range(8)]

    def stream_w(dram):
        wlist = [(dram, k * 128, 128, 0, 1024, wbands[k]) for k in range(8)]
        return emit_stream_weights(p, wlist, stage,
                                   start_wait=[("pe", p.val("pe"))])

    def streamed_proj(tag, outd, epi_mid, rot, outdt):
        store_so = {0: 0, 1: 0}

        def epi(of, tci, c0, c1, ps_ap, pev):
            r = rot[of % 2]
            epi_mid(of, c0, c1, ps_ap, pev, r,
                    extra_wait=[("so", store_so[of % 2])])
            dvv = p.val("dv")
            if c1 == T:
                p.op("gpsimd", lambda e, _of=of, _r=r: e.dma_start(
                    out=outd[_of * 128:(_of + 1) * 128, :], in_=_r[:, 0:T]),
                    inc="so", inc_n=16, wait=[("dv", dvv)])
                store_so[of % 2] = p.val("so")
            return [("dv", dvv)]
        emit_proj(p, tag, lambda k, of: wbands[k][:, of * 128:(of + 1) * 128],
                  xn, NKT, NKT, T, epi, pool,
                  rhs_ready=[("dv", max(stream_w_dv[0], xn7_dv))])

    stream_w_dv = [0]

    def mid_copy(of, c0, c1, ps_ap, pev, r, extra_wait):
        p.op("vector", lambda e: e.tensor_copy(out=r[:, c0:c1], in_=ps_ap),
             inc="dv", wait=[("pe", pev)] + extra_wait)

    stream_w_dv[0] = stream_w(wq)
    streamed_proj("dq", qo, mid_copy, rotb, BF16)
    stream_w_dv[0] = stream_w(wv)
    streamed_proj("dvp", vo, mid_copy, rotf, F32)

    stream_w_dv[0] = stream_w(wk)

    def mid_kn(of, c0, c1, ps_ap, pev, r, extra_wait):
        p.op("vector", lambda e: e.tensor_copy(out=r[:, c0:c1], in_=ps_ap),
             inc="dv", wait=[("pe", pev)] + extra_wait)
        if c1 != T:
            return
        p.op("scalar", lambda e, _r=r: e.activation(
            out=sqt[:, 0:T], in_=_r[:, 0:T], func=AF.Square),
            inc="ac", wait=[("dv", p.val("dv")), ("ac", p.val("ac"))])
        acv = p.val("ac")
        for (d0, d1) in tchunks(T):
            p.op("tensor", lambda e, _d0=d0, _d1=d1: e.matmul(
                out=pn2[:, 0:_d1 - _d0], lhsT=sel_n[:], rhs=sqt[:, _d0:_d1],
                start=True, stop=True),
                inc="pe", wait=[("ac", acv), ("dv", p.val("dv"))])
            pev2 = p.val("pe")
            p.vop(lambda e, _d0=d0, _d1=d1: e.tensor_copy(
                out=nnb[0:2, _d0:_d1], in_=pn2[0:2, 0:_d1 - _d0]),
                wait=[("pe", pev2)])
        dvv = p.val("dv")
        p.op("scalar", lambda e: e.activation(
            out=nnb[0:2, 0:T], in_=nnb[0:2, 0:T], func=AF.Ln), inc="ac",
            wait=[("dv", dvv)])
        p.op("scalar", lambda e: e.activation(
            out=nnb[0:2, 0:T], in_=nnb[0:2, 0:T], func=AF.Exp, scale=0.5),
            inc="ac")
        acv = p.val("ac")
        p.vop(lambda e: e.tensor_scalar_max(
            out=nnb[0:2, 0:T], in0=nnb[0:2, 0:T], scalar1=1e-12),
            wait=[("ac", acv)])
        p.vop(lambda e: e.reciprocal(out=nnb[0:2, 0:T], in_=nnb[0:2, 0:T]))
        rsn_dv = p.val("dv")
        for (d0, d1) in tchunks(T):
            p.op("tensor", lambda e, _d0=d0, _d1=d1: e.matmul(
                out=pbc2[:, 0:_d1 - _d0], lhsT=sel_b[:], rhs=nnb[0:2, _d0:_d1],
                start=True, stop=True),
                inc="pe", wait=[("dv", max(rsn_dv, p.val("dv")))])
            pev2 = p.val("pe")
            p.op("vector", lambda e, _r=r, _d0=d0, _d1=d1: e.tensor_tensor(
                out=_r[:, _d0:_d1], in0=_r[:, _d0:_d1],
                in1=pbc2[:, 0:_d1 - _d0], op=ALU.mult),
                inc="dv", wait=[("pe", pev2)])

    streamed_proj("dk", kno, mid_kn, rotf, F32)

    for wdrm, bname, outd in ((wa, "ba", lao), (war, "bar", laro)):
        stream_w_dv[0] = stream_w(wdrm)

        def mid_la(of, c0, c1, ps_ap, pev, r, extra_wait, _b=bname):
            w = c1 - c0
            p.vop(lambda e: e.tensor_scalar_add(
                out=scr0[:, 0:w], in0=ps_ap,
                scalar1=smalls[_b][:, of:of + 1]),
                wait=[("pe", pev)] + extra_wait)
            p.vop(lambda e: e.tensor_scalar_mul(
                out=scr1[:, 0:w], in0=scr0[:, 0:w], scalar1=-1.0))
            p.vop(lambda e: e.tensor_tensor(
                out=scr1[:, 0:w], in0=scr0[:, 0:w], in1=scr1[:, 0:w],
                op=ALU.min))
            dvv = p.val("dv")
            p.op("scalar", lambda e: e.activation(
                out=scr1[:, 0:w], in_=scr1[:, 0:w], func=AF.Exp),
                inc="ac", wait=[("dv", dvv)])
            p.op("scalar", lambda e: e.activation(
                out=scr1[:, 0:w], in_=scr1[:, 0:w], func=AF.Ln, bias=1.0),
                inc="ac")
            acv = p.val("ac")
            p.vop(lambda e: e.tensor_scalar_min(
                out=scr0[:, 0:w], in0=scr0[:, 0:w], scalar1=0.0),
                wait=[("ac", acv)])
            p.vop(lambda e, _c0=c0, _c1=c1, _r=r: e.tensor_tensor(
                out=_r[:, _c0:_c1], in0=scr0[:, 0:w], in1=scr1[:, 0:w],
                op=ALU.subtract))

        streamed_proj("dla", outd, mid_la, rotf, F32)

    wbg = [warena[:, 2048 + k * 32:2048 + (k + 1) * 32] for k in range(8)]
    wlist = []
    for k in range(8):
        wlist.append((wb, k * 128, 128, 0, 16, wbg[k][:, 0:16]))
        wlist.append((wg, k * 128, 128, 0, 16, wbg[k][:, 16:32]))
    wbg_dv = emit_stream_weights(p, wlist, stage,
                                 start_wait=[("pe", p.val("pe"))])
    for half, bname, r0 in ((0, "bb", 0), (1, "bg", 64)):
        for (c0, c1) in tchunks(T):
            ps_t, frees = pool[0]
            for k in range(NKT):
                p.op("tensor", lambda e, _k=k, _h=half, _c0=c0, _c1=c1, _ps=ps_t: e.matmul(
                    out=_ps[0:16, 0:_c1 - _c0],
                    lhsT=wbg[_k][:, _h * 16:(_h + 1) * 16],
                    rhs=xn[_k][:, _c0:_c1], start=(_k == 0), stop=(_k == NKT - 1)),
                    inc="pe" if k == NKT - 1 else None,
                    wait=(frees + [("dv", wbg_dv)]) if k == 0 else [])
            pev = p.val("pe")
            p.vop(lambda e, _r0=r0, _c0=c0, _c1=c1, _ps=ps_t, _b=bname: e.tensor_scalar_add(
                out=bgsb[_r0:_r0 + 16, _c0:_c1],
                in0=_ps[0:16, 0:_c1 - _c0], scalar1=smalls[_b][:]),
                wait=[("pe", pev)])
            pool[0] = [ps_t, [("dv", p.val("dv"))]]
    dvv = p.val("dv")
    p.op("scalar", lambda e: e.activation(
        out=bgsb[0:16, 0:T], in_=bgsb[0:16, 0:T], func=AF.Sigmoid), inc="ac",
        wait=[("dv", dvv)])
    p.op("scalar", lambda e: e.activation(
        out=bgsb[64:80, 0:T], in_=bgsb[64:80, 0:T], func=AF.Sigmoid), inc="ac")
    acv = p.val("ac")
    p.op("gpsimd", lambda e: e.dma_start(out=bgo[0:16, :], in_=bgsb[0:16, 0:T]),
         inc="so", inc_n=16, wait=[("ac", acv)])
    p.op("gpsimd", lambda e: e.dma_start(out=bgo[16:32, :], in_=bgsb[64:80, 0:T]),
         inc="so", inc_n=16)

    p.wait("gpsimd", "so", p.val("so"))
    return p.build()


DC = 128
L10_STAGE = 99
NCH = 4096 // DC


def build_L10(c_start=0, c_end=NCH):
    p = Prog("L10")
    S = 4096
    knf_d = p.din("kn_fm", [512, S], F32)
    qf_d = p.din("q_fm", [512, S], BF16)
    laf_d = p.din("la_fm", [512, S], F32)
    laRf_d = p.din("laR_fm", [512, S], F32)
    vtm_d = p.din("v_tm", [S, 512], F32)
    btm_d = p.din("bg_tm", [S, 16], F32)
    msu_d = p.din("mask_su2", [128, 256], BF16)
    msi_d = p.din("mask_si2", [128, 256], BF16)
    eyed = p.din("eye128b", [128, 128], BF16)
    eyefd = p.din("eye128f", [128, 128], F32)
    sin_d = p.din("states_in", [512, 128], F32)
    o_out = p.dout("o_tm", [S, 512], F32)
    s_out = p.dout("states_out", [512, 128], F32)
    for sname in ("li", "lb2", "pe", "dv", "ac", "so"):
        p.sem(sname)
    p.init_consts()

    G = 4
    laf = [p.sb(f"laf{g}", [128, DC], F32) for g in range(G)]
    laRf = [p.sb(f"laRf{g}", [128, DC], F32) for g in range(G)]
    knf = [p.sb(f"knf{g}", [128, DC], F32) for g in range(G)]
    qf = [p.sb(f"qf{g}", [128, DC], BF16) for g in range(G)]
    vtm = p.sb("vtm", [128, 512], F32)
    bgcol = p.sb("bgcol", [128, 16], F32)
    E1 = [p.sb(f"E1_{g}", [128, DC], F32) for g in range(G)]
    E1m = [p.sb(f"E1m_{g}", [128, DC], F32) for g in range(G)]
    E1s = [p.sb(f"E1s_{g}", [128, DC], F32) for g in range(G)]
    kt = [p.sb(f"kt{h}", [64, DC], BF16) for h in range(8)]
    kh = [p.sb(f"kh{h}", [64, DC], BF16) for h in range(8)]
    kp = [p.sb(f"kp{h}", [64, DC], BF16) for h in range(8)]
    qt = [p.sb(f"qt{h}", [64, DC], BF16) for h in range(8)]
    zget = p.sb("zget", [128, DC], F32)
    msu = p.sb("msu", [128, 256], BF16)
    msi = p.sb("msi", [128, 256], BF16)
    eyeb = p.sb("eyeb", [128, 128], BF16)
    eyef = p.sb("eyef", [128, 128], F32)
    XT = p.sb("XTb", [128, 7 * 256], F32)
    Xc = [p.sb(f"Xc{i}", [128, 256], F32) for i in range(2)]
    Zf = p.sb("Zf", [128, 128], F32)
    Wt = p.sb("Wt", [128, 128], BF16)
    rf = [p.sb(f"rf{hp}", [128, 128], F32) for hp in range(4)]
    khtm = p.sb("khtm", [128, 128], BF16)
    MqTS = p.sb("MqTS", [128, 256], BF16)
    MpTS = p.sb("MpTS", [128, 256], BF16)
    oac = [p.sb(f"oac{hp}", [128, 128], F32) for hp in range(4)]
    ssc = p.sb("ssc", [64, 64], F32)
    Sf = [p.sb(f"Sf{h}", [64, 128], F32) for h in range(8)]
    Sb = [p.sb(f"Sb{h}", [64, 128], BF16) for h in range(8)]

    pMMf = p.ps("pMMf", [128, 512], F32)
    pMM = pMMf[:, 0:256]
    pMTf = p.ps("pMTf", [128, 512], F32)
    pMT = pMTf[:, 0:256]
    pZf = p.ps("pZf", [128, 512], F32)
    pZ = pZf[:, 0:128]
    pSTf = p.ps("pSTf", [128, 512], F32)
    pST = pSTf[0:64, 0:128]
    pTRf = p.ps("pTRf", [128, 1024], BF16)
    pTR = pTRf[:, 0:128]

    p.op("vector", lambda e: e.memset(zget[:], 0.0), inc="dv")
    p.op("sync", lambda e: e.dma_start(out=msu[:], in_=msu_d[:, :]), inc="lb2", inc_n=16)
    p.op("sync", lambda e: e.dma_start(out=msi[:], in_=msi_d[:, :]), inc="lb2", inc_n=16)
    p.op("sync", lambda e: e.dma_start(out=eyeb[:], in_=eyed[:, :]), inc="lb2", inc_n=16)
    p.op("sync", lambda e: e.dma_start(out=eyef[:], in_=eyefd[:, :]), inc="lb2", inc_n=16)
    for h in range(8):
        p.op("sync", lambda e, _h=h: e.dma_start(
            out=Sf[_h][:], in_=sin_d[_h * 64:(_h + 1) * 64, :]),
            inc="lb2", inc_n=16)
    lb2v = p.val("lb2")
    for eng in ("vector", "tensor", "scalar"):
        p.wait(eng, "lb2", lb2v)
    for h in range(8):
        p.vop(lambda e, _h=h: e.tensor_copy(out=Sb[_h][:], in_=Sf[_h][:]))

    ostore_all = [0]

    def hs(buf, g, hh):
        return buf[g][hh * 64:(hh + 1) * 64, :]

    for ci in range(c_start, c_end):
        t0 = ci * DC
        guard = [("dv", p.val("dv")), ("pe", p.val("pe"))]
        for g in range(G):
            p.op("sync", lambda e, _g=g, _t0=t0: e.dma_start(
                out=laf[_g][:], in_=laf_d[_g * 128:(_g + 1) * 128, _t0:_t0 + DC]),
                inc="li", inc_n=16, wait=guard if g == 0 else [])
            p.op("sync", lambda e, _g=g, _t0=t0: e.dma_start(
                out=laRf[_g][:], in_=laRf_d[_g * 128:(_g + 1) * 128, _t0:_t0 + DC]),
                inc="li", inc_n=16)
            p.op("sync", lambda e, _g=g, _t0=t0: e.dma_start(
                out=knf[_g][:], in_=knf_d[_g * 128:(_g + 1) * 128, _t0:_t0 + DC]),
                inc="li", inc_n=16)
            p.op("sync", lambda e, _g=g, _t0=t0: e.dma_start(
                out=qf[_g][:], in_=qf_d[_g * 128:(_g + 1) * 128, _t0:_t0 + DC]),
                inc="li", inc_n=16)
        p.op("sync", lambda e, _t0=t0: e.dma_start(
            out=vtm[:], in_=vtm_d[_t0:_t0 + DC, :]), inc="li", inc_n=16)
        p.op("sync", lambda e, _t0=t0: e.dma_start(
            out=bgcol[:], in_=btm_d[_t0:_t0 + DC, :]), inc="li", inc_n=16)
        li_v = p.val("li")
        for eng in ("vector", "scalar", "tensor"):
            p.wait(eng, "li", li_v)

        for chain in (0, 1):
            lsrc = laf if chain == 0 else laRf
            sc0 = 0 if chain == 0 else 64
            gc0 = 0 if chain == 0 else 8
            for g in range(G):
                p.vop(lambda e, _g=g, _l=lsrc: e.tensor_tensor_scan(
                    out=E1[_g][:], data0=_l[_g][:], data1=zget[:],
                    initial=0.0, op0=ALU.add, op1=ALU.add))
                p.vop(lambda e, _g=g: e.tensor_scalar_max(
                    out=E1[_g][:], in0=E1[_g][:], scalar1=-80.0))
                if chain == 0:
                    p.vop(lambda e, _g=g: e.tensor_copy(
                        out=E1s[_g][:, 1:DC], in_=E1[_g][:, 0:DC - 1]))
                    p.vop(lambda e, _g=g: e.memset(E1s[_g][:, 0:1], 0.0))
            es_dv = p.val("dv")
            for g in range(G):
                p.op("scalar", lambda e, _g=g: e.activation(
                    out=E1m[_g][:], in_=E1[_g][:], func=AF.Exp, scale=-1.0),
                    inc="ac", wait=[("dv", es_dv)] if g == 0 else [])
                if chain == 0:
                    p.op("scalar", lambda e, _g=g: e.activation(
                        out=E1s[_g][:], in_=E1s[_g][:], func=AF.Exp), inc="ac")
                p.op("scalar", lambda e, _g=g: e.activation(
                    out=E1[_g][:], in_=E1[_g][:], func=AF.Exp), inc="ac")
            exp_ac = p.val("ac")
            for h in range(8):
                g, b0 = h // 2, (h % 2) * 64
                p.vop(lambda e, _g=g, _b=b0, _h=h: e.tensor_tensor(
                    out=kt[_h][:], in0=knf[_g][_b:_b + 64, :],
                    in1=E1[_g][_b:_b + 64, :], op=ALU.mult),
                    wait=[("ac", exp_ac)] if h == 0 else [])
                p.vop(lambda e, _g=g, _b=b0, _h=h: e.tensor_tensor(
                    out=kh[_h][:], in0=knf[_g][_b:_b + 64, :],
                    in1=E1m[_g][_b:_b + 64, :], op=ALU.mult))
                if chain == 0:
                    p.vop(lambda e, _g=g, _b=b0, _h=h: e.tensor_tensor(
                        out=kp[_h][:], in0=knf[_g][_b:_b + 64, :],
                        in1=E1s[_g][_b:_b + 64, :], op=ALU.mult))
                p.vop(lambda e, _g=g, _b=b0, _h=h: e.tensor_tensor(
                    out=qt[_h][:], in0=qf[_g][_b:_b + 64, :],
                    in1=E1[_g][_b:_b + 64, :], op=ALU.mult))
            prep_dv = p.val("dv")
            if L10_STAGE < 2:
                continue

            for hp in range(4):
                g = hp
                for hh in range(2):
                    p.op("tensor", lambda e, _hp=hp, _hh=hh: e.matmul(
                        out=pMM[:, _hh * 128:(_hh + 1) * 128],
                        lhsT=kh[2 * _hp + _hh][:], rhs=kt[2 * _hp + _hh][:],
                        start=(_hh == 0), stop=(_hh == 1)),
                        inc="pe" if hh == 1 else None,
                        wait=[("dv", prep_dv)] if hh == 0 else [])
                pev = p.val("pe")
                for hh in range(2):
                    p.vop(lambda e, _hh=hh, _hp=hp, _c=gc0: e.scalar_tensor_tensor(
                        out=XT[:, _hh * 128:(_hh + 1) * 128],
                        in0=pMM[:, _hh * 128:(_hh + 1) * 128],
                        scalar=bgcol[:, _c + 2 * _hp + _hh:_c + 2 * _hp + _hh + 1],
                        in1=msu[:, 0:128], op0=ALU.mult, op1=ALU.mult),
                        wait=[("pe", pev)] if hh == 0 else [])
                if L10_STAGE < 3:
                    continue
                for hh in range(2):
                    dvv = p.val("dv")
                    p.op("tensor", lambda e, _hh=hh: e.transpose(
                        out=pZf[:, 0:128], in_=XT[:, _hh * 128:(_hh + 1) * 128],
                        identity=eyef[:]), inc="pe", wait=[("dv", dvv)])
                    pev = p.val("pe")
                    p.vop(lambda e, _hh=hh: e.tensor_copy(
                        out=Xc[0][:, _hh * 128:(_hh + 1) * 128], in_=pZf[:, 0:128]),
                        wait=[("pe", pev)])
                if L10_STAGE < 4:
                    continue
                for k in range(1, 7):
                    xprev = Xc[(k - 1) % 2]
                    xnext = Xc[k % 2]
                    dvv = p.val("dv")
                    for hh in range(2):
                        p.op("tensor", lambda e, _hh=hh, _k=k, _xp=xprev: e.matmul(
                            out=pMM[:, _hh * 128:(_hh + 1) * 128],
                            lhsT=XT[:, (_k - 1) * 256 + _hh * 128:(_k - 1) * 256 + (_hh + 1) * 128],
                            rhs=_xp[:, _hh * 128:(_hh + 1) * 128],
                            start=(_hh == 0), stop=(_hh == 1)),
                            inc="pe" if hh == 1 else None,
                            wait=[("dv", dvv)] if hh == 0 else [])
                        p.op("tensor", lambda e, _hh=hh, _k=k, _xp=xprev: e.matmul(
                            out=pMT[:, _hh * 128:(_hh + 1) * 128],
                            lhsT=_xp[:, _hh * 128:(_hh + 1) * 128],
                            rhs=XT[:, (_k - 1) * 256 + _hh * 128:(_k - 1) * 256 + (_hh + 1) * 128],
                            start=(_hh == 0), stop=(_hh == 1)),
                            inc="pe" if hh == 1 else None)
                    pev = p.val("pe")
                    p.vop(lambda e, _xn=xnext: e.tensor_copy(
                        out=_xn[:], in_=pMM[:]), wait=[("pe", pev)])
                    p.vop(lambda e, _k=k: e.tensor_copy(
                        out=XT[:, _k * 256:(_k + 1) * 256], in_=pMT[:]))
                sq_dv = p.val("dv")
                if L10_STAGE < 5:
                    continue
                for hh in range(2):
                    p.op("tensor", lambda e, _hp=hp, _hh=hh, _s=sc0: e.matmul(
                        out=pZ[:, _hh * 64:(_hh + 1) * 64],
                        lhsT=kt[2 * _hp + _hh][:],
                        rhs=Sb[2 * _hp + _hh][:, _s:_s + 64],
                        start=(_hh == 0), stop=(_hh == 1)),
                        inc="pe" if hh == 1 else None,
                        wait=[("dv", sq_dv)] if hh == 0 else [])
                pev = p.val("pe")
                cont = (lambda _hp=hp: vtm[:, _hp * 128:(_hp + 1) * 128]) if chain == 0 \
                    else (lambda _hp=hp: rf[_hp][:])
                p.vop(lambda e, _c=cont: e.tensor_tensor(
                    out=Zf[:], in0=_c(), in1=pZ[:], op=ALU.subtract),
                    wait=[("pe", pev)])
                for k in range(7):
                    dvv = p.val("dv")
                    for hh in range(2):
                        p.op("tensor", lambda e, _hh=hh, _k=k: e.matmul(
                            out=pZ[:, _hh * 64:(_hh + 1) * 64],
                            lhsT=XT[:, _k * 256 + _hh * 128:_k * 256 + (_hh + 1) * 128],
                            rhs=Zf[:, _hh * 64:(_hh + 1) * 64],
                            start=(_hh == 0), stop=(_hh == 1)),
                            inc="pe" if hh == 1 else None,
                            wait=[("dv", dvv)] if hh == 0 else [])
                    pev = p.val("pe")
                    p.vop(lambda e, _k=k: e.tensor_tensor(
                        out=Zf[:], in0=Zf[:], in1=pZ[:],
                        op=ALU.subtract if _k == 0 else ALU.add),
                        wait=[("pe", pev)])
                if L10_STAGE < 6:
                    continue
                for hh in range(2):
                    p.vop(lambda e, _hh=hh, _hp=hp, _c=gc0: e.tensor_single_scalar(
                        out=Wt[:, _hh * 64:(_hh + 1) * 64],
                        in_=Zf[:, _hh * 64:(_hh + 1) * 64],
                        scalar=bgcol[:, _c + 2 * _hp + _hh:_c + 2 * _hp + _hh + 1],
                        op=ALU.mult))
                w_dv = p.val("dv")
                if chain == 0:
                    for hh in range(2):
                        p.op("tensor", lambda e, _hp=hp, _hh=hh: e.matmul(
                            out=pMM[:, _hh * 128:(_hh + 1) * 128],
                            lhsT=kh[2 * _hp + _hh][:], rhs=kp[2 * _hp + _hh][:],
                            start=(_hh == 0), stop=(_hh == 1)),
                            inc="pe" if hh == 1 else None,
                            wait=[("dv", w_dv)] if hh == 0 else [])
                    pev = p.val("pe")
                    p.vop(lambda e: e.tensor_tensor(
                        out=MpTS[:], in0=pMM[:], in1=msu[:], op=ALU.mult),
                        wait=[("pe", pev)])
                    mp_dv = p.val("dv")
                    for hh in range(2):
                        p.op("tensor", lambda e, _hh=hh: e.matmul(
                            out=pZ[:, _hh * 64:(_hh + 1) * 64],
                            lhsT=MpTS[:, _hh * 128:(_hh + 1) * 128],
                            rhs=Wt[:, _hh * 64:(_hh + 1) * 64],
                            start=(_hh == 0), stop=False),
                            inc=None, wait=[("dv", mp_dv)] if hh == 0 else [])
                        p.op("tensor", lambda e, _hp=hp, _hh=hh, _s=sc0: e.matmul(
                            out=pZ[:, _hh * 64:(_hh + 1) * 64],
                            lhsT=kp[2 * _hp + _hh][:],
                            rhs=Sb[2 * _hp + _hh][:, _s:_s + 64],
                            start=False, stop=(_hh == 1)),
                            inc="pe" if hh == 1 else None)
                    pev = p.val("pe")
                    p.vop(lambda e, _hp=hp: e.tensor_tensor(
                        out=rf[_hp][:], in0=vtm[:, _hp * 128:(_hp + 1) * 128],
                        in1=pZ[:], op=ALU.subtract), wait=[("pe", pev)])
                    p.vop(lambda e, _hp=hp: e.tensor_scalar_min(
                        out=rf[_hp][:], in0=rf[_hp][:], scalar1=1.0))
                    p.vop(lambda e, _hp=hp: e.tensor_scalar_max(
                        out=rf[_hp][:], in0=rf[_hp][:], scalar1=-1.0))
                if L10_STAGE < 7:
                    continue
                dvv = p.val("dv")
                for hh in range(2):
                    p.op("tensor", lambda e, _hp=hp, _hh=hh: e.matmul(
                        out=pMM[:, _hh * 128:(_hh + 1) * 128],
                        lhsT=kh[2 * _hp + _hh][:], rhs=qt[2 * _hp + _hh][:],
                        start=(_hh == 0), stop=(_hh == 1)),
                        inc="pe" if hh == 1 else None,
                        wait=[("dv", dvv)] if hh == 0 else [])
                pev = p.val("pe")
                p.vop(lambda e: e.tensor_tensor(
                    out=MqTS[:], in0=pMM[:], in1=msi[:], op=ALU.mult),
                    wait=[("pe", pev)])
                mq_dv = p.val("dv")
                for hh in range(2):
                    p.op("tensor", lambda e, _hh=hh: e.matmul(
                        out=pZ[:, _hh * 64:(_hh + 1) * 64],
                        lhsT=MqTS[:, _hh * 128:(_hh + 1) * 128],
                        rhs=Wt[:, _hh * 64:(_hh + 1) * 64],
                        start=(_hh == 0), stop=False),
                        inc=None, wait=[("dv", mq_dv)] if hh == 0 else [])
                    p.op("tensor", lambda e, _hp=hp, _hh=hh, _s=sc0: e.matmul(
                        out=pZ[:, _hh * 64:(_hh + 1) * 64],
                        lhsT=qt[2 * _hp + _hh][:],
                        rhs=Sb[2 * _hp + _hh][:, _s:_s + 64],
                        start=False, stop=(_hh == 1)),
                        inc="pe" if hh == 1 else None)
                pev = p.val("pe")
                if chain == 0:
                    p.vop(lambda e, _hp=hp: e.tensor_copy(
                        out=oac[_hp][:], in_=pZ[:]),
                        wait=[("pe", pev), ("so", ostore_all[0])])
                else:
                    p.vop(lambda e, _hp=hp: e.tensor_tensor(
                        out=oac[_hp][:], in0=oac[_hp][:], in1=pZ[:],
                        op=ALU.add), wait=[("pe", pev)])
                if L10_STAGE < 8:
                    continue
                for hh in range(2):
                    dvv = p.val("dv")
                    p.op("tensor", lambda e, _hp=hp, _hh=hh: e.transpose(
                        out=pTR[:, 0:64], in_=kh[2 * _hp + _hh][:],
                        identity=eyeb[0:64, 0:64]), inc="pe",
                        wait=[("dv", dvv)])
                    pev = p.val("pe")
                    p.vop(lambda e, _hh=hh: e.tensor_copy(
                        out=khtm[:, _hh * 64:(_hh + 1) * 64], in_=pTR[:, 0:64]),
                        wait=[("pe", pev)])
                kh_dv = p.val("dv")
                for hh in range(2):
                    p.op("tensor", lambda e, _hh=hh: e.matmul(
                        out=pST[0:64, _hh * 64:(_hh + 1) * 64],
                        lhsT=khtm[:, _hh * 64:(_hh + 1) * 64],
                        rhs=Wt[:, _hh * 64:(_hh + 1) * 64],
                        start=(_hh == 0), stop=(_hh == 1)),
                        inc="pe" if hh == 1 else None,
                        wait=[("dv", kh_dv)] if hh == 0 else [])
                pev = p.val("pe")
                for hh in range(2):
                    h = 2 * hp + hh
                    p.vop(lambda e, _h=h, _hh=hh, _s=sc0: e.tensor_tensor(
                        out=ssc[:], in0=Sf[_h][:, _s:_s + 64],
                        in1=pST[0:64, _hh * 64:(_hh + 1) * 64], op=ALU.add),
                        wait=[("pe", pev)] if hh == 0 else [])
                    p.vop(lambda e, _h=h, _hh=hh, _s=sc0, _g=g: e.tensor_single_scalar(
                        out=Sf[_h][:, _s:_s + 64],
                        in_=ssc[:],
                        scalar=E1[_g][(_hh) * 64:(_hh) * 64 + 64, DC - 1:DC],
                        op=ALU.mult))
                    p.vop(lambda e, _h=h, _s=sc0: e.tensor_copy(
                        out=Sb[_h][:, _s:_s + 64], in_=Sf[_h][:, _s:_s + 64]))
        for hp in range(4):
            dvv = p.val("dv")
            p.op("gpsimd", lambda e, _hp=hp, _t0=t0: e.dma_start(
                out=o_out[_t0:_t0 + DC, _hp * 128:(_hp + 1) * 128],
                in_=oac[_hp][:]), inc="so", inc_n=16, wait=[("dv", dvv)])
        ostore_all[0] = p.val("so")

    dvv = p.val("dv")
    for h in range(8):
        p.op("gpsimd", lambda e, _h=h: e.dma_start(
            out=s_out[_h * 64:(_h + 1) * 64, :], in_=Sf[_h][:]),
            inc="so", inc_n=16, wait=[("dv", dvv)])
    p.wait("gpsimd", "so", p.val("so"))
    return p.build()


def build_L11():
    p = Prog("L11")
    x6d = p.din("x6T", [1024, T], F32)
    od = p.din("oT", [1024, T], F32)
    wod = p.din("woT", [1024, 1024], F32)
    win = p.din("ffn_win", [1024, 64], F32)
    binb = p.din("ffn_bin", [64, 1], F32)
    w1 = p.din("ffn_w1", [64, 16], F32)
    b1 = p.din("ffn_b1", [64, 16], F32)
    w2 = p.din("ffn_w2", [64, 16], F32)
    b2 = p.din("ffn_b2", [64, 1], F32)
    wout = p.din("ffn_wout", [64, 1024], F32)
    bout = p.din("ffn_bout", [128, 8], F32)
    x8o = p.dout("x8T", [1024, T], F32)
    for sname in ("li", "lw0", "lw1", "lb2", "pe", "dv", "ac", "so"):
        p.sem(sname)
    p.init_consts()

    acc = [p.sb(f"acc{k}", [128, T], F32) for k in range(NKT)]
    xn = [p.sb(f"xn{k}", [128, T], BF16) for k in range(NKT)]
    ob = [p.sb(f"ob{k}", [128, T], BF16) for k in range(NKT)]
    stage = [p.sb(f"stg{i}", [128, 1024], F32) for i in range(2)]
    warena = p.sb("warenaL11", [128, 8192 + 2048], BF16)
    sg = p.sb("sg", [64, T], F32)
    hsc = p.sb("hsc", [64, T], F32)
    gacc = p.sb("gacc", [64, T], F32)
    gbf = p.sb("gbf", [64, T], BF16)
    scrA = p.sb("scrA", [128, TC], F32)
    o_t = p.sb("o_t", [128, T], F32)
    smalls = {}
    for nm, shp, dr in (("binb", [64, 1], binb), ("w1", [64, 16], w1),
                        ("b1", [64, 16], b1), ("w2", [64, 16], w2),
                        ("b2", [64, 1], b2), ("bout", [128, 8], bout)):
        smalls[nm] = p.sb("s_" + nm, shp, F32)
        p.op("sync", lambda e, _d=dr, _n=nm: e.dma_start(
            out=smalls[_n][:], in_=_d[:, :]), inc="lb2", inc_n=16)
    for eng in ("vector", "tensor", "scalar"):
        p.wait(eng, "lb2", p.val("lb2"))
    ln = LNScratch(p, "ln", sq_bufs=[o_t[:, 0:TC], o_t[:, TC:2 * TC]],
                   absb_buf=o_t[:, 2 * TC:3 * TC])
    pool = [[p.ps(f"pp{i}", [128, TC], F32), []] for i in range(2)]

    emit_load_fm(p, x6d, acc, 1024, T, sem="li")
    x6_li0 = p.val("li")
    emit_ln(p, ln, acc, acc, T, [("li", x6_li0)], residual=True)
    for k in range(NKT):
        guard = [("dv", p.val("dv"))]
        p.op("sync", lambda e, _k=k: e.dma_start(
            out=o_t[:], in_=od[_k * 128:(_k + 1) * 128, :]),
            inc="li", inc_n=16, wait=guard)
        liv = p.val("li")
        p.op("vector", lambda e, _k=k: e.tensor_copy(out=ob[_k][:], in_=o_t[:]),
             inc="dv", wait=[("li", liv)])
    o_dv = p.val("dv")
    x6_li = p.val("li")

    wob = [warena[:, k * 1024:(k + 1) * 1024] for k in range(8)]
    wlist = [(wod, k * 128, 128, 0, 1024, wob[k]) for k in range(8)]
    wo_dv = emit_stream_weights(p, wlist, stage)

    def epi_wo(of, tci, c0, c1, ps_ap, pev):
        p.op("vector", lambda e: e.tensor_tensor(
            out=acc[of][:, c0:c1], in0=acc[of][:, c0:c1], in1=ps_ap,
            op=ALU.add), inc="dv", wait=[("pe", pev), ("li", x6_li)])
        return [("dv", p.val("dv"))]

    emit_proj(p, "wo", lambda k, of: wob[k][:, of * 128:(of + 1) * 128],
              ob, NKT, NKT, T, epi_wo, pool, rhs_ready=[("dv", max(o_dv, wo_dv))])
    x7_dv = p.val("dv")

    emit_ln(p, ln, acc, xn, T, [("dv", x7_dv)])
    xn_dv = p.val("dv")

    winb = [warena[:, 8192 + k * 64:8192 + (k + 1) * 64] for k in range(8)]
    woutffn = warena[:, 8192 + 512:8192 + 512 + 1024]
    wlist = [(win, k * 128, 128, 0, 64, winb[k]) for k in range(8)]
    wlist += [(wout, 0, 64, 0, 1024, woutffn)]
    wffn_dv = emit_stream_weights(p, wlist, stage,
                                  start_wait=[("pe", p.val("pe"))])
    for (c0, c1) in tchunks(T):
        ps_t, frees = pool[0]
        for k in range(NKT):
            p.op("tensor", lambda e, _k=k, _c0=c0, _c1=c1, _ps=ps_t: e.matmul(
                out=_ps[0:64, 0:_c1 - _c0], lhsT=winb[_k][:],
                rhs=xn[_k][:, _c0:_c1], start=(_k == 0), stop=(_k == NKT - 1)),
                inc="pe" if k == NKT - 1 else None,
                wait=(frees + [("dv", wffn_dv), ("dv", xn_dv)]) if k == 0 else [])
        pev = p.val("pe")
        p.op("vector", lambda e, _c0=c0, _c1=c1, _ps=ps_t: e.tensor_scalar_add(
            out=sg[0:64, _c0:_c1], in0=_ps[0:64, 0:_c1 - _c0],
            scalar1=smalls["binb"][:]), inc="dv", wait=[("pe", pev)])
        pool[0] = [ps_t, [("dv", p.val("dv"))]]
    for i in range(16):
        p.op("scalar", lambda e, _i=i: e.activation(
            out=hsc[0:64, 0:T], in_=sg[0:64, 0:T], func=AF.Relu,
            scale=smalls["w1"][:, _i:_i + 1], bias=smalls["b1"][:, _i:_i + 1]),
            inc="ac", wait=[("dv", p.val("dv"))])
        acv = p.val("ac")
        if i == 0:
            p.vop(lambda e: e.tensor_single_scalar(
                out=gacc[0:64, 0:T], in_=hsc[0:64, 0:T],
                scalar=smalls["w2"][:, 0:1], op=ALU.mult), wait=[("ac", acv)])
        else:
            p.vop(lambda e, _i=i: e.scalar_tensor_tensor(
                out=gacc[0:64, 0:T], in0=hsc[0:64, 0:T],
                scalar=smalls["w2"][:, _i:_i + 1], in1=gacc[0:64, 0:T],
                op0=ALU.mult, op1=ALU.add), wait=[("ac", acv)])
    p.vop(lambda e: e.tensor_scalar_add(
        out=gacc[0:64, 0:T], in0=gacc[0:64, 0:T], scalar1=smalls["b2"][:]))
    p.vop(lambda e: e.tensor_copy(out=gbf[0:64, 0:T], in_=gacc[0:64, 0:T]))
    g_dv = p.val("dv")

    def epi_ffo(of, tci, c0, c1, ps_ap, pev):
        w = c1 - c0
        p.vop(lambda e: e.tensor_scalar_add(
            out=scrA[:, 0:w], in0=ps_ap, scalar1=smalls["bout"][:, of:of + 1]),
            wait=[("pe", pev)])
        p.op("vector", lambda e: e.tensor_tensor(
            out=acc[of][:, c0:c1], in0=acc[of][:, c0:c1], in1=scrA[:, 0:w],
            op=ALU.add), inc="dv")
        return [("dv", p.val("dv"))]

    emit_proj(p, "ffo", lambda k, of: woutffn[0:64, of * 128:(of + 1) * 128],
              lambda k, c0, c1: gbf[0:64, c0:c1], 1, NKT, T, epi_ffo, pool,
              rhs_ready=[("dv", g_dv)])
    emit_store_fm(p, x8o, acc, 1024, T, wait=[("dv", p.val("dv"))])
    p.wait("gpsimd", "so", p.val("so"))
    return p.build()



def shard_tokens_fm(x):
    out = []
    for c in range(NCORES):
        b, h = c // 2, c % 2
        out.append(np.ascontiguousarray(x[b, h * T:(h + 1) * T, :].T))
    return out


def bias_cols(b):
    return np.ascontiguousarray(b.reshape(8, 128).T)


def run_launch(nc, in_maps):
    return run_bass_kernel_spmd(nc, in_maps, core_ids=list(range(NCORES))).results


_PROGS = {}
DEBUG_STAGES = None


def _prog(name, builder):
    if name not in _PROGS:
        _PROGS[name] = builder()
    return _PROGS[name]


def kernel(**inputs):
    import ml_dtypes
    BF = ml_dtypes.bfloat16
    d = inputs
    AC = np.ascontiguousarray
    eye8 = np.eye(8, dtype=np.float32)
    eye128f = np.eye(128, dtype=np.float32)
    eye128b = eye128f.astype(BF)
    sel_n = np.zeros((128, 2), np.float32)
    sel_n[0:64, 0] = 1
    sel_n[64:128, 1] = 1
    sel_b = AC(sel_n.T)
    msu = np.triu(np.ones((128, 128), np.float32), 1).astype(BF)
    msi = np.triu(np.ones((128, 128), np.float32), 0).astype(BF)
    msu2 = AC(np.concatenate([msu, msu], axis=1))
    msi2 = AC(np.concatenate([msi, msi], axis=1))

    x0 = d["L0_attn_a_x"]
    shards = shard_tokens_fm(x0)

    nc = _prog("L1", build_L1)
    in1 = [{"xT": shards[c],
            "wqT": AC(d["L0_attn_a_Wq"].T), "wkT": AC(d["L0_attn_a_Wk"].T),
            "wvT": AC(d["L0_attn_a_Wv"].T),
            "bq": bias_cols(d["L0_attn_a_bq"]), "bk": bias_cols(d["L0_attn_a_bk"]),
            "bv": bias_cols(d["L0_attn_a_bv"])} for c in range(NCORES)]
    r1 = run_launch(nc, in1)
    if DEBUG_STAGES is not None:
        DEBUG_STAGES["r1"] = r1

    nc = _prog("L2", build_L2)
    in2 = []
    for c in range(NCORES):
        b, hg = c // 2, c % 2
        kf = np.concatenate([r1[2 * b]["KfT"], r1[2 * b + 1]["KfT"]], axis=1)
        vf = np.concatenate([r1[2 * b]["VT"], r1[2 * b + 1]["VT"]], axis=1)
        qf = np.concatenate([r1[2 * b]["QfT"], r1[2 * b + 1]["QfT"]], axis=1)
        in2.append({"Kf_tm": AC(kf[hg * 512:(hg + 1) * 512, :].T),
                    "V_tm": AC(vf[hg * 512:(hg + 1) * 512, :].T),
                    "Qf_fm": AC(qf[hg * 512:(hg + 1) * 512, :]),
                    "eye8": eye8})
    r2 = run_launch(nc, in2)

    nc = _prog("L3", build_L3)
    in3 = []
    for c in range(NCORES):
        b, half = c // 2, c % 2
        ofull = np.concatenate([r2[2 * b]["oT"], r2[2 * b + 1]["oT"]], axis=0)
        in3.append({"xT": shards[c],
                    "oT": AC(ofull[:, half * T:(half + 1) * T]),
                    "woT": AC(d["L0_attn_a_Wo"].T), "bo": bias_cols(d["L0_attn_a_bo"]),
                    "gwT": AC(d["L0_ffn_gate_w"].T),
                    "lb": d["L0_ffn_lb_bias"].reshape(8, 1), "eye8": eye8})
    r3 = run_launch(nc, in3)
    if DEBUG_STAGES is not None:
        DEBUG_STAGES["r3"] = r3

    def gstats_of(res, key):
        g = np.zeros((8, 16), np.float32)
        for c in range(NCORES):
            g[:, c] = res[c][key][:, 0]
            g[:, 8 + c] = res[c][key][:, 1]
        return g

    moe_w = {"egwT": AC(d["L0_ffn_expert_gate_w"].transpose(0, 2, 1)),
             "euwT": AC(d["L0_ffn_expert_up_w"].transpose(0, 2, 1)),
             "edwT": AC(d["L0_ffn_expert_down_w"].transpose(0, 2, 1)),
             "sgwT": AC(d["L0_ffn_shared_gate_w"].T),
             "suwT": AC(d["L0_ffn_shared_up_w"].T),
             "sdwT": AC(d["L0_ffn_shared_down_w"].T)}

    nc = _prog("L4", build_L4)
    gstats1 = gstats_of(r3, "gpart")
    in4 = []
    for c in range(NCORES):
        half = c % 2
        mask = np.zeros((1, T), BF)
        if half == 1:
            mask[0, 1920:] = 1
        in4.append({"x1T": r3[c]["x1T"], "xn2T": r3[c]["xn2T"],
                    "m1fm": AC(r3[c]["m_tm"].T).astype(BF),
                    "gstats1": gstats1,
                    "wvT": AC(d["L0_attn_b_Wv"].T), "woT": AC(d["L0_attn_b_Wo"].T),
                    "gwT": AC(d["L0_ffn_gate_w"].T),
                    "lb": d["L0_ffn_lb_bias"].reshape(8, 1),
                    "eye8": eye8, "mask_row": mask, **moe_w})
    r4 = run_launch(nc, in4)
    if DEBUG_STAGES is not None:
        DEBUG_STAGES["r4"] = r4

    nc = _prog("L5", build_L5)
    gstats2 = gstats_of(r4, "gpart2")
    mamba_small = {}
    for i in range(2):
        mamba_small[i] = {
            "convw": AC(d["L1_attn_a_conv_w"][i].reshape(16, 128, 4)
                        .transpose(1, 0, 2).reshape(128, 64)),
            "convb": AC(d["L1_attn_a_conv_b"][i].reshape(16, 128).T),
            "bdt": AC(d["L1_attn_a_b_dt"][i].reshape(16, 128).T),
        }
    in5 = []
    for c in range(NCORES):
        b, half = c // 2, c % 2
        m2 = np.concatenate([r4[2 * b]["m2_tm"], r4[2 * b + 1]["m2_tm"]], axis=0).T
        if half == 0:
            hx3 = np.zeros((1024, 3), np.float32)
            hxn = np.zeros((1024, 3), BF)
            hm = np.zeros((8, 3), np.float32)
        else:
            hx3 = r4[2 * b]["x3T"][:, T - 3:T]
            hxn = r4[2 * b]["xn4T"][:, T - 3:T]
            hm = m2[:, T - 3:T]
        in5.append({"x3T": AC(np.concatenate([hx3, r4[c]["x3T"]], axis=1)),
                    "xn4T": AC(np.concatenate([hxn, r4[c]["xn4T"]], axis=1)),
                    "m2fm": AC(np.concatenate(
                        [hm, m2[:, half * T:(half + 1) * T]], axis=1)).astype(BF),
                    "gstats2": gstats2,
                    "winT": AC(d["L1_attn_a_W_in"][0].T),
                    "wxT": AC(d["L1_attn_a_W_x"][0].T),
                    "wdtT": AC(d["L1_attn_a_W_dt"][0].T),
                    "eye8": eye8, **mamba_small[0], **moe_w})
    r5 = run_launch(nc, in5)
    if DEBUG_STAGES is not None:
        DEBUG_STAGES["r5"] = r5

    nc = _prog("L6", build_L6)

    def scan_inputs(res, ukey, dtkey, bckey, layer):
        dt_full, u_full, bc_full = {}, {}, {}
        for b in range(4):
            dt_full[b] = np.concatenate([res[2 * b][dtkey], res[2 * b + 1][dtkey]], axis=1)
            u_full[b] = np.concatenate([res[2 * b][ukey], res[2 * b + 1][ukey]], axis=1)
            bc_full[b] = np.concatenate([res[2 * b][bckey], res[2 * b + 1][bckey]], axis=1)
        ins = []
        for c in range(NCORES):
            b, dh = c // 2, c % 2
            ins.append({
                "dt_dh": AC(dt_full[b][dh * 1024:(dh + 1) * 1024]),
                "u_dh": AC(u_full[b][dh * 1024:(dh + 1) * 1024]),
                "Brep": AC(np.repeat(bc_full[b][0:16], 128, axis=0)),
                "Crep": AC(np.repeat(bc_full[b][16:32], 128, axis=0)),
                "alog": AC(d["L1_attn_a_A_log"][layer][dh * 1024:(dh + 1) * 1024]),
                "dcol": AC(d["L1_attn_a_D"][layer][dh * 1024:(dh + 1) * 1024]
                           .reshape(8, 128).T)})
        return ins, u_full

    in6, _ = scan_inputs(r5, "u", "dt", "bc", 0)
    r6 = run_launch(nc, in6)
    if DEBUG_STAGES is not None:
        DEBUG_STAGES["r6"] = r6

    nc7 = _prog("L7", build_L7)
    yfull = {b: np.concatenate([r6[2 * b]["y"], r6[2 * b + 1]["y"]], axis=0)
             for b in range(4)}
    zfull = {b: np.concatenate([r5[2 * b]["z"], r5[2 * b + 1]["z"]], axis=1)
             for b in range(4)}
    in7 = []
    for c in range(NCORES):
        b, half = c // 2, c % 2

        def halo(a):
            own = a[:, half * T:(half + 1) * T]
            h = np.zeros((a.shape[0], 3), a.dtype) if half == 0 \
                else a[:, T - 3:T]
            return AC(np.concatenate([h, own], axis=1))
        in7.append({"yT": halo(yfull[b]), "zT": halo(zfull[b]),
                    "woutT": AC(d["L1_attn_a_W_out"][0].T),
                    "winT": AC(d["L1_attn_a_W_in"][1].T),
                    "wxT": AC(d["L1_attn_a_W_x"][1].T),
                    "wdtT": AC(d["L1_attn_a_W_dt"][1].T),
                    **mamba_small[1]})
    r7 = run_launch(nc7, in7)
    if DEBUG_STAGES is not None:
        DEBUG_STAGES["r7"] = r7

    in8, _ = scan_inputs(r7, "u", "dt", "bc", 1)
    r8 = run_launch(nc, in8)
    if DEBUG_STAGES is not None:
        DEBUG_STAGES["r8"] = r8

    nc = _prog("L9", build_L9)
    y2full = {b: np.concatenate([r8[2 * b]["y"], r8[2 * b + 1]["y"]], axis=0)
              for b in range(4)}
    z2full = {b: np.concatenate([r7[2 * b]["z"], r7[2 * b + 1]["z"]], axis=1)
              for b in range(4)}
    ffn_w = {"ffn_win": AC(d["L1_ffn_Win"]),
             "ffn_bin": d["L1_ffn_bin_"].reshape(64, 1),
             "ffn_w1": d["L1_ffn_W1"], "ffn_b1": d["L1_ffn_b1"],
             "ffn_w2": d["L1_ffn_W2"], "ffn_b2": d["L1_ffn_b2"].reshape(64, 1),
             "ffn_wout": d["L1_ffn_Wout"],
             "ffn_bout": bias_cols(d["L1_ffn_bout"])}
    in9 = []
    for c in range(NCORES):
        b, half = c // 2, c % 2
        sl = np.s_[:, half * T:(half + 1) * T]
        in9.append({"y2T": AC(y2full[b][sl]), "z2T": AC(z2full[b][sl]),
                    "x4T": r5[c]["x4T"],
                    "wout1T": AC(d["L1_attn_a_W_out"][1].T),
                    "wqT": AC(d["L1_attn_b_Wq"].T), "wkT": AC(d["L1_attn_b_Wk"].T),
                    "wvT": AC(d["L1_attn_b_Wv"].T), "waT": AC(d["L1_attn_b_Wa"].T),
                    "waRT": AC(d["L1_attn_b_WaR"].T),
                    "ba": bias_cols(d["L1_attn_b_ba"]),
                    "baR": bias_cols(d["L1_attn_b_baR"]),
                    "wbT": AC(d["L1_attn_b_Wb"].T), "wgT": AC(d["L1_attn_b_Wg"].T),
                    "bb": d["L1_attn_b_bb"].reshape(16, 1),
                    "bg": d["L1_attn_b_bg"].reshape(16, 1),
                    "ln_w": bias_cols(d["L1_attn_b_ln_w"]),
                    "ln_b": bias_cols(d["L1_attn_b_ln_b"]),
                    "sel_n": sel_n, "sel_b": sel_b, **ffn_w})
    r9 = run_launch(nc, in9)
    if DEBUG_STAGES is not None:
        DEBUG_STAGES["r9"] = r9

    nc = _prog("L10", build_L10)
    dfull = {nm: {b: np.concatenate([r9[2 * b][nm], r9[2 * b + 1][nm]], axis=1)
                  for b in range(4)}
             for nm in ("kn", "v", "q", "la", "laR", "betagamma")}
    in10 = []
    for c in range(NCORES):
        b, hg = c // 2, c % 2
        rs = np.s_[hg * 512:(hg + 1) * 512, :]
        bgf = dfull["betagamma"][b]
        bg_tm = np.concatenate([bgf[hg * 8:(hg + 1) * 8].T,
                                bgf[16 + hg * 8:16 + (hg + 1) * 8].T], axis=1)
        in10.append({"kn_fm": AC(dfull["kn"][b][rs]),
                     "q_fm": AC(dfull["q"][b][rs]),
                     "la_fm": AC(dfull["la"][b][rs]),
                     "laR_fm": AC(dfull["laR"][b][rs]),
                     "v_tm": AC(dfull["v"][b][rs].T),
                     "bg_tm": AC(bg_tm),
                     "mask_su2": msu2, "mask_si2": msi2,
                     "eye128b": eye128b, "eye128f": eye128f,
                     "states_in": np.zeros((512, 128), np.float32)})
    r10 = run_launch(nc, in10)
    if DEBUG_STAGES is not None:
        DEBUG_STAGES["r10"] = r10

    nc = _prog("L11", build_L11)
    o_fm = {}
    for b in range(4):
        o_fm[b] = np.concatenate(
            [AC(r10[2 * b]["o_tm"].T), AC(r10[2 * b + 1]["o_tm"].T)], axis=0)
    in11 = []
    for c in range(NCORES):
        b, half = c // 2, c % 2
        in11.append({"x6T": r9[c]["x6T"],
                     "oT": AC(o_fm[b][:, half * T:(half + 1) * T]),
                     "woT": AC(d["L1_attn_b_Wo"].T), **ffn_w})
    r11 = run_launch(nc, in11)

    out = np.empty((4, 4096, 1024), np.float32)
    for c in range(NCORES):
        b, half = c // 2, c % 2
        out[b, half * T:(half + 1) * T, :] = r11[c]["x8T"].T
    return out
